# version 17
# baseline (speedup 1.0000x reference)
import contextlib

import numpy as np

import concourse.bass as bass
import concourse.mybir as mybir
from concourse.bass_utils import run_bass_kernel_spmd

F32 = mybir.dt.float32
BF16 = mybir.dt.bfloat16
U8 = mybir.dt.uint8
AF = mybir.ActivationFunctionType
ALU = mybir.AluOpType
AX = mybir.AxisListType

D = 1024
HID = 4096
ND = D // 128
NO = HID // 128
NBH = 4
TWO_PI_N = float(2.0 * np.pi / 1024.0)

ENGS = ("tensor", "vector", "scalar", "sync", "gpsimd")
POOLS = {"dS": ("sync", 6), "dA": ("scalar", 4), "dG": ("gpsimd", 4)}


class Plan:

    def __init__(self, sems):
        self.q = {e: [] for e in ENGS}
        self.sems = sems
        self.cnt = {s: 0 for s in sems}
        self.pool_idx = {p: 0 for p in POOLS}

    def op(self, eng, meth, *args, inc=(), **kw):
        inc = tuple(inc)

        def f(e, meth=meth, args=args, kw=kw, inc=inc):
            ins = getattr(e, meth)(*args, **kw)
            for s, d in inc:
                ins = ins.then_inc(self.sems[s], d)

        self.q[eng].append(f)
        for s, d in inc:
            self.cnt[s] += d

    def dma(self, pool, out_ap, in_ap, **kw):
        eng, n = POOLS[pool]
        s = f"{pool}{self.pool_idx[pool] % n}"
        self.pool_idx[pool] += 1
        self.wait(eng, s, self.cnt[s])
        self.op(eng, "dma_start", out_ap, in_ap, inc=[(s, 16)], **kw)
        return (s, self.cnt[s])

    def wait(self, eng, sem, val):
        def f(e, sem=sem, val=val):
            e.wait_ge(self.sems[sem], val)

        self.q[eng].append(f)

    def wait_tok(self, eng, tok):
        self.wait(eng, tok[0], tok[1])

    def wait_pool(self, eng, pool):
        _, n = POOLS[pool]
        for i in range(n):
            s = f"{pool}{i}"
            if self.cnt[s]:
                self.wait(eng, s, self.cnt[s])

    def drain(self, eng):
        self.q[eng].append(lambda e: e.drain())


def build_nc(T_TOT, TC):
    NTC = T_TOT // TC
    NTT = TC // 128

    nc = bass.Bass("TRN2", target_bir_lowering=False)

    x_in = nc.dram_tensor("x_in", [T_TOT, D], F32, kind="ExternalInput")
    wqT = nc.dram_tensor("wqT", [ND, 128, D], F32, kind="ExternalInput")
    wkT = nc.dram_tensor("wkT", [ND, 128, D], F32, kind="ExternalInput")
    wvT = nc.dram_tensor("wvT", [ND, 128, D], F32, kind="ExternalInput")
    w1t = nc.dram_tensor("w1t", [NO, 128, ND * 128], F32, kind="ExternalInput")
    w2t = nc.dram_tensor("w2t", [NO, 128, ND * 128], F32, kind="ExternalInput")
    h1t = nc.dram_tensor("h1t", [NO, 128, NO * 128], F32, kind="ExternalInput")
    h2t = nc.dram_tensor("h2t", [NO, 128, NO * 128], F32, kind="ExternalInput")
    w3t = nc.dram_tensor("w3t", [2, 128, NO * 512], F32,
                         kind="ExternalInput")
    s1_in = nc.dram_tensor("s1_in", [128, NO], F32,
                          kind="ExternalInput")
    s2_in = nc.dram_tensor("s2_in", [128, NO], F32,
                          kind="ExternalInput")
    sh1_in = nc.dram_tensor("sh1_in", [128, NO], F32,
                          kind="ExternalInput")
    sh2_in = nc.dram_tensor("sh2_in", [128, NO], F32,
                          kind="ExternalInput")
    s3_in = nc.dram_tensor("s3_in", [1, D], F32, kind="ExternalInput")

    y_out = nc.dram_tensor("y", [T_TOT, D], F32, kind="ExternalOutput")

    asp = {}
    for nm in ("q", "k", "v"):
        for half in ("hi", "lo"):
            asp[nm, half] = nc.dram_tensor(f"a{nm}_{half}",
                                           [2 * NBH, 128, D], BF16,
                                           kind="Internal")
    tfw = {}
    for nm in ("cf", "sf"):
        for half in ("hi", "lo"):
            tfw[nm, half] = nc.dram_tensor(f"{nm}_{half}", [ND, 128, 512],
                                           BF16, kind="Internal")
    tin = {}
    for nm in ("ci", "si"):
        for half in ("hi", "lo"):
            tin[nm, half] = nc.dram_tensor(f"{nm}_{half}", [NBH, 128, D],
                                           BF16, kind="Internal")
    attn_s = nc.dram_tensor("attn_s", [T_TOT, D], F32, kind="Internal")

    x_t = x_in.ap().rearrange("(tt p) d -> tt p d", p=128)
    y_t = y_out.ap().rearrange("(tt p) d -> tt p d", p=128)
    attn_t = attn_s.ap().rearrange("(tt p) d -> tt p d", p=128)

    ctx = contextlib.ExitStack()

    sb = {}

    def sb_alloc(name, shape, dtype):
        sb[name] = ctx.enter_context(nc.sbuf_tensor(name, shape, dtype))
        return sb[name]

    sb_alloc("idn", [128, 128], BF16)
    sb_alloc("xti0", [128, D], F32)
    sb_alloc("xti1", [128, D], F32)
    sb_alloc("xnf", [128, D], F32)
    sb_alloc("xh_tok", [128, D], BF16)
    sb_alloc("xl_tok", [128, D], BF16)
    sb_alloc("xnhT", [128, ND * TC], BF16)
    sb_alloc("xnlT", [128, ND * TC], BF16)
    sb_alloc("t1", [128, 512], F32)
    sb_alloc("t2", [128, 512], F32)
    sb_alloc("t3", [128, 512], F32)
    sb_alloc("t4", [128, 512], F32)
    sb_alloc("atile", [128, D], F32)
    sb_alloc("zeros", [128, 512], F32)
    sb_alloc("carry", [128, 8], F32)
    sb_alloc("cols", [128, 32], F32)
    sb_alloc("scl", [128, 4 * NO], F32)
    sb_alloc("s3bc", [128, D], F32)
    sb_alloc("kk", [128, 512], F32)
    sb_alloc("nn", [128, D], F32)
    sb_alloc("ncol", [128, 1], F32)
    sb_alloc("pcol", [128, 1], F32)
    sb_alloc("wcol", [128, 1], F32)
    sb_alloc("b16a", [128, 512], BF16)
    sb_alloc("b16b", [128, 512], BF16)
    sb_alloc("b16c", [128, D], BF16)
    sb_alloc("b16d", [128, D], BF16)

    POOLB = 140 * 1024
    pool_h = ctx.enter_context(nc.sbuf_tensor("pool", [128, POOLB], U8))
    pool_base = nc.lookup_mloc(pool_h).addr

    def pool_at(name, off_kib, shape, dtype):
        h = nc.virtual_sbuf_tensor(name, shape, dtype)
        nc._tensor(h.name, shape, dtype, type="SB")
        mloc = nc.lookup_mloc(h)
        mloc.allocated = True
        mloc.addr = pool_base + off_kib * 1024
        sb[name] = h
        return h

    pool_at("uh", 0, [128, ND * TC], BF16)
    pool_at("ul", 8, [128, ND * TC], BF16)
    pool_at("cfh", 16, [128, ND * 512], BF16)
    pool_at("cfl", 24, [128, ND * 512], BF16)
    pool_at("sfh", 32, [128, ND * 512], BF16)
    pool_at("sfl", 40, [128, ND * 512], BF16)
    pool_at("wsl0h", 48, [128, D], BF16)
    pool_at("wsl0l", 50, [128, D], BF16)
    pool_at("wsl1h", 52, [128, D], BF16)
    pool_at("wsl1l", 54, [128, D], BF16)
    pool_at("fq", 56, [128, 2 * NBH * TC], F32)
    pool_at("fk", 72, [128, 2 * NBH * TC], F32)
    pool_at("fv", 88, [128, 2 * NBH * TC], F32)
    pool_at("Mpl", 104, [128, 2 * NBH * TC], F32)
    pool_at("zh", 120, [128, 2 * NBH * TC], BF16)
    pool_at("zl", 128, [128, 2 * NBH * TC], BF16)
    pool_at("wfh", 56, [128, ND * D], BF16)
    pool_at("wfl", 72, [128, ND * D], BF16)
    pool_at("wgh", 88, [128, ND * D], BF16)
    pool_at("wgl", 104, [128, ND * D], BF16)
    pool_at("cih", 56, [128, NBH * D], BF16)
    pool_at("cil", 64, [128, NBH * D], BF16)
    pool_at("sih", 72, [128, NBH * D], BF16)
    pool_at("sil", 80, [128, NBH * D], BF16)
    pool_at("arA_h", 0, [128, NO * TC], BF16)
    pool_at("arA_l", 32, [128, NO * TC], BF16)
    pool_at("arB_h", 64, [128, NO * TC], BF16)
    pool_at("arB_l", 96, [128, NO * TC], BF16)
    pool_at("w3pre", 64, [128, 2 * NO * 512], BF16)
    pool_at("xpf0", 128, [128, D], F32)
    pool_at("apf0", 132, [128, D], F32)
    pool_at("fsl0", 128, [128, NO * 128], BF16)
    pool_at("fsl1", 136, [128, D], BF16)
    pool_at("fsl2", 138, [128, D], BF16)

    ps = [ctx.enter_context(nc.psum_tensor(f"ps{i}", [128, 512], F32))
          for i in range(6)]
    pstA = ctx.enter_context(nc.psum_tensor("pstA", [128, 1024], BF16))
    pstB = ctx.enter_context(nc.psum_tensor("pstB", [128, 1024], BF16))

    semnames = ["tM", "vV", "sC", "gP"]
    for p, (_, np_) in POOLS.items():
        semnames += [f"{p}{i}" for i in range(np_)]
    sems = {s: ctx.enter_context(nc.semaphore(s)) for s in semnames}
    P = Plan(sems)

    def split_fp32_to_bf16(src_ap, hi_ap, lo_ap, inc=()):
        P.op("vector", "tensor_copy", hi_ap, src_ap)
        P.drain("vector")
        P.op("vector", "tensor_tensor", lo_ap, src_ap, hi_ap, ALU.subtract,
             inc=inc)

    ps_free = [("vV", 0)] * 6

    def matmul_group(bank, pieces):
        P.wait("tensor", *ps_free[bank])
        width = pieces[0][1].shape[-1]
        n = len(pieces)
        for i, (l, r) in enumerate(pieces):
            P.op("tensor", "matmul", ps[bank].ap()[:, 0:width], l, r,
                 start=(i == 0), stop=(i == n - 1),
                 inc=[("tM", 1)] if i == n - 1 else ())
        return P.cnt["tM"]

    ps_rr = [0]

    def next_bank():
        i = ps_rr[0] % 6
        ps_rr[0] += 1
        return i

    def bank_evac_done(bank, sem="vV"):
        ps_free[bank] = (sem, P.cnt[sem])

    P.op("gpsimd", "iota", sb["t1"].ap()[:, 0:128], [[1, 128]], base=0,
         channel_multiplier=-1, allow_small_or_imprecise_dtypes=True)
    P.op("gpsimd", "iota", sb["kk"].ap(), [[1, 512]], base=0,
         channel_multiplier=0, allow_small_or_imprecise_dtypes=True)
    P.op("gpsimd", "iota", sb["nn"].ap(), [[1, D]], base=0,
         channel_multiplier=0, allow_small_or_imprecise_dtypes=True)
    P.op("gpsimd", "iota", sb["pcol"].ap(), [[0, 1]], base=0,
         channel_multiplier=1, allow_small_or_imprecise_dtypes=True,
         inc=[("gP", 1)])
    P.wait("vector", "gP", P.cnt["gP"])
    P.op("vector", "tensor_single_scalar", sb["idn"].ap(),
         sb["t1"].ap()[:, 0:128], 0.0, ALU.is_equal)
    P.op("vector", "memset", sb["carry"].ap(), 0.0)
    P.op("vector", "memset", sb["cols"].ap()[:, 28:29], 1e-5)
    P.op("vector", "memset", sb["cols"].ap()[:, 29:30], float(np.pi))
    P.op("vector", "memset", sb["cols"].ap()[:, 30:31], -float(np.pi))
    P.op("vector", "memset", sb["zeros"].ap(), 1.0, inc=[("vV", 1)])

    for i, src in enumerate((s1_in, s2_in, sh1_in, sh2_in)):
        P.dma("dS", sb["scl"].ap()[:, i * NO:(i + 1) * NO], src.ap())
    dv = P.dma("dS", sb["s3bc"].ap()[0:1, :], s3_in.ap())
    n = 1
    while n < 128:
        P.wait_tok("sync", dv)
        dv = P.dma("dS", sb["s3bc"].ap()[n:2 * n, :], sb["s3bc"].ap()[0:n, :])
        n *= 2

    def emit_P1(tc, overlap=False):
        tt0 = tc * NTT
        if overlap:
            xb_tiles = tuple(sb["xti0"] if t % 2 == 0 else sb["xti1"]
                             for t in range(NTT))
            P.wait("scalar", "tM", tm_after_p6)
        else:
            xb_tiles = (sb["xti0"], sb["xti1"], sb["xpf0"], sb["apf0"])
        dvs = []
        for tt in range(NTT):
            xbuf = xb_tiles[tt]
            if (not overlap) or tt < 2:
                P.wait("sync", "vV", P.cnt["vV"])
                P.wait("sync", "sC", P.cnt["sC"])
                dvs.append(P.dma("dS", xbuf.ap(), x_t[tt0 + tt]))
            else:
                dvs.append(None)
        for tt in range(NTT):
            xbuf = xb_tiles[tt]
            if overlap and tt >= 2 and dvs[tt] is None:
                P.wait("sync", "vV", P.cnt["vV"])
                P.wait("sync", "sC", P.cnt["sC"])
                dvs[tt] = P.dma("dS", xbuf.ap(), x_t[tt0 + tt])
            P.wait_tok("vector", dvs[tt])
            P.drain("vector")
            P.op("vector", "reduce_sum", sb["cols"].ap()[:, tt:tt + 1],
                 xbuf.ap(), axis=AX.X, inc=[("vV", 1)])
            P.wait_tok("scalar", dvs[tt])
            if tt == 0:
                P.wait("scalar", "vV", P.cnt["vV"] - 1)
            P.drain("scalar")
            P.op("scalar", "activation", sb["xnf"].ap(), xbuf.ap(),
                 AF.Square, accum_out=sb["cols"].ap()[:, 4 + tt:5 + tt],
                 inc=[("sC", 1)])
        P.wait("vector", "sC", P.cnt["sC"])
        P.drain("vector")
        P.op("vector", "tensor_single_scalar", sb["cols"].ap()[:, 8:8 + NTT],
             sb["cols"].ap()[:, 0:NTT], 1.0 / D, ALU.mult)
        P.op("vector", "tensor_single_scalar", sb["cols"].ap()[:, 12:12 + NTT],
             sb["cols"].ap()[:, 4:4 + NTT], 1.0 / D, ALU.mult)
        P.drain("vector")
        P.op("vector", "tensor_tensor", sb["cols"].ap()[:, 16:16 + NTT],
             sb["cols"].ap()[:, 8:8 + NTT], sb["cols"].ap()[:, 8:8 + NTT],
             ALU.mult)
        P.drain("vector")
        P.op("vector", "tensor_tensor", sb["cols"].ap()[:, 12:12 + NTT],
             sb["cols"].ap()[:, 12:12 + NTT], sb["cols"].ap()[:, 16:16 + NTT],
             ALU.subtract, inc=[("vV", 1)])
        P.wait("scalar", "vV", P.cnt["vV"])
        P.drain("scalar")
        P.op("scalar", "activation", sb["cols"].ap()[:, 16:16 + NTT],
             sb["cols"].ap()[:, 12:12 + NTT], AF.Sqrt,
             bias=sb["cols"].ap()[:, 28:29], inc=[("sC", 1)])
        P.wait("vector", "sC", P.cnt["sC"])
        P.drain("vector")
        P.op("vector", "reciprocal", sb["cols"].ap()[:, 20:20 + NTT],
             sb["cols"].ap()[:, 16:16 + NTT])
        P.drain("vector")
        P.op("vector", "scalar_tensor_tensor", sb["cols"].ap()[:, 24:24 + NTT],
             sb["cols"].ap()[:, 8:8 + NTT], -1.0,
             sb["cols"].ap()[:, 20:20 + NTT], ALU.mult, ALU.mult,
             inc=[("vV", 1)])
        vv_stats = P.cnt["vV"]
        P.wait("scalar", "vV", vv_stats)
        for tt in range(NTT):
            xbuf = xb_tiles[tt]
            if overlap:
                P.wait("sync", "vV", P.cnt["vV"])
                P.wait("sync", "sC", P.cnt["sC"])
                rdv = P.dma("dS", xbuf.ap(), x_t[tt0 + tt])
                P.wait_tok("scalar", rdv)
            P.wait("scalar", "vV", P.cnt["vV"])
            P.drain("scalar")
            P.op("scalar", "activation", sb["xnf"].ap(), xbuf.ap(),
                 AF.Identity, bias=sb["cols"].ap()[:, 24 + tt:25 + tt],
                 scale=sb["cols"].ap()[:, 20 + tt:21 + tt], inc=[("sC", 1)])
            P.wait("vector", "sC", P.cnt["sC"])
            P.drain("vector")
            split_fp32_to_bf16(sb["xnf"].ap(), sb["xh_tok"].ap(),
                               sb["xl_tok"].ap(), inc=[("vV", 1)])
            vv_split = P.cnt["vV"]
            P.wait("tensor", "vV", vv_split)
            sc_ev = P.cnt["sC"]
            i = 0
            for tok, dstT in ((sb["xh_tok"], sb["xnhT"]),
                              (sb["xl_tok"], sb["xnlT"])):
                for dc in range(ND):
                    bankT = pstA if i % 2 == 0 else pstB
                    if i >= 2:
                        P.wait("tensor", "sC", sc_ev + i - 1)
                    P.op("tensor", "transpose", bankT.ap()[:, 0:128],
                         tok.ap()[:, dc * 128:(dc + 1) * 128],
                         sb["idn"].ap(), inc=[("tM", 1)])
                    P.wait("scalar", "tM", P.cnt["tM"])
                    P.op("scalar", "copy",
                         dstT.ap()[:, dc * TC + tt * 128:
                                   dc * TC + (tt + 1) * 128],
                         bankT.ap()[:, 0:128], inc=[("sC", 1)])
                    i += 1
            P.wait("tensor", "sC", P.cnt["sC"])



    def emit_fold_prep(toks_tbl_sb):
        st_toks = []
        projs = (("k", wkT), ("v", wvT), ("q", wqT))
        pairs = ((sb["wfh"], sb["wfl"]), (sb["wgh"], sb["wgl"]))
        plane_free = [0, 0]
        vv_wd = {}

        def emit_wsplit(pi):
            nm, wt = projs[pi]
            ph, pl = pairs[pi % 2]
            P.wait("vector", "tM", plane_free[pi % 2])
            for ec in range(ND):
                xb = sb["xti0"] if ec % 2 == 0 else sb["xti1"]
                P.wait("sync", "vV", P.cnt["vV"])
                dv = P.dma("dS", xb.ap(), wt.ap()[ec])
                P.wait_tok("vector", dv)
                P.drain("vector")
                split_fp32_to_bf16(
                    xb.ap(), ph.ap()[:, ec * D:(ec + 1) * D],
                    pl.ap()[:, ec * D:(ec + 1) * D], inc=[("vV", 1)])
            vv_wd[pi] = P.cnt["vV"]

        emit_wsplit(0)
        for pi, (nm, wt) in enumerate(projs):
            if pi + 1 < len(projs):
                emit_wsplit(pi + 1)
            ph, pl = pairs[pi % 2]
            P.wait("tensor", "vV", vv_wd[pi])
            for t_ in toks_tbl_sb:
                P.wait_tok("tensor", t_)
            for dc in range(ND):
                for is_im in (0, 1):
                    th = sb["sfh" if is_im else "cfh"]
                    tl = sb["sfl" if is_im else "cfl"]
                    bank = next_bank()
                    pieces = []
                    for ec in range(ND):
                        wh = ph.ap()[:, ec * D + dc * 128:
                                     ec * D + (dc + 1) * 128]
                        wl = pl.ap()[:, ec * D + dc * 128:
                                     ec * D + (dc + 1) * 128]
                        tt_h = th.ap()[:, ec * 512:(ec + 1) * 512]
                        tt_l = tl.ap()[:, ec * 512:(ec + 1) * 512]
                        pieces += [(wh, tt_h), (wl, tt_h), (wh, tt_l)]
                    tm = matmul_group(bank, pieces)
                    P.wait("vector", "tM", tm)
                    for t_ in st_toks[-8:]:
                        P.wait_tok("vector", t_)
                    split_fp32_to_bf16(ps[bank].ap()[:, 0:512],
                                       sb["b16a"].ap(), sb["b16b"].ap(),
                                       inc=[("vV", 1)])
                    bank_evac_done(bank)
                    P.wait("scalar", "vV", P.cnt["vV"])
                    for bj in range(4):
                        bc = is_im * NBH + bj
                        st_toks.append(P.dma(
                            "dA",
                            asp[nm, "hi"].ap()[bc][:, dc * 128:
                                                   (dc + 1) * 128],
                            sb["b16a"].ap()[:, bj * 128:(bj + 1) * 128]))
                        st_toks.append(P.dma(
                            "dA",
                            asp[nm, "lo"].ap()[bc][:, dc * 128:
                                                   (dc + 1) * 128],
                            sb["b16b"].ap()[:, bj * 128:(bj + 1) * 128]))
            plane_free[pi % 2] = P.cnt["tM"]
        return [(f"dA{i}", P.cnt[f"dA{i}"]) for i in range(POOLS["dA"][1])
                if P.cnt[f"dA{i}"]]

    def fold_steps(tmp, mask, steps):
        for sft in steps:
            P.drain("vector")
            P.op("vector", "tensor_single_scalar", mask, tmp, sft, ALU.is_ge)
            P.drain("vector")
            P.op("vector", "scalar_tensor_tensor", tmp, mask, -sft, tmp,
                 ALU.mult, ALU.add)

    def fold_to_signed(tmp, mask):
        fold_steps(tmp, mask, (512.0,))
        P.drain("vector")
        P.op("vector", "scalar_tensor_tensor", tmp, mask, -512.0, tmp,
             ALU.mult, ALU.add)

    P.drain("vector")
    P.op("vector", "tensor_single_scalar", sb["t4"].ap(), sb["kk"].ap(),
         128.0, ALU.mult)
    fold_steps(sb["t4"].ap(), sb["t2"].ap(),
               (32768.0, 16384.0, 8192.0, 4096.0, 2048.0, 1024.0))
    P.drain("vector")
    P.op("vector", "tensor_single_scalar", sb["xti0"].ap(), sb["nn"].ap(),
         128.0, ALU.mult)
    fold_steps(sb["xti0"].ap(), sb["atile"].ap(),
               (65536.0, 32768.0, 16384.0, 8192.0, 4096.0, 2048.0, 1024.0))

    for is_sf in (0, 1):
        for ec in range(ND):
            if ec == 0:
                P.drain("vector")
                P.op("vector", "tensor_scalar", sb["t1"].ap(), sb["kk"].ap(),
                     sb["pcol"].ap(), 0.0 if is_sf else 256.0, ALU.mult,
                     ALU.add)
                fold_steps(sb["t1"].ap(), sb["t2"].ap(),
                           (65536.0, 32768.0, 16384.0, 8192.0, 4096.0,
                            2048.0, 1024.0))
            else:
                P.drain("vector")
                P.op("vector", "tensor_tensor", sb["t1"].ap(), sb["t1"].ap(),
                     sb["t4"].ap(), ALU.add)
                fold_steps(sb["t1"].ap(), sb["t2"].ap(), (1024.0,))
            P.op("vector", "engine_nop", inc=[("vV", 1)])
            P.wait("scalar", "vV", P.cnt["vV"])
            P.drain("scalar")
            P.op("scalar", "activation", sb["t3"].ap(), sb["t1"].ap(),
                 AF.Sin, scale=(TWO_PI_N if is_sf else -TWO_PI_N),
                 bias=(sb["cols"].ap()[:, 30:31] if is_sf else
                       sb["cols"].ap()[:, 29:30]),
                 inc=[("sC", 1)])
            P.wait("vector", "sC", P.cnt["sC"])
            if is_sf:
                P.op("vector", "tensor_copy", sb["cols"].ap()[:, 8:9],
                     sb["pcol"].ap())
                for sft in (64.0, 32.0, 16.0, 8.0, 4.0, 2.0):
                    P.drain("vector")
                    P.op("vector", "tensor_single_scalar",
                         sb["cols"].ap()[:, 10:11], sb["cols"].ap()[:, 8:9],
                         sft, ALU.is_ge)
                    P.drain("vector")
                    P.op("vector", "scalar_tensor_tensor",
                         sb["cols"].ap()[:, 8:9], sb["cols"].ap()[:, 10:11],
                         -sft, sb["cols"].ap()[:, 8:9], ALU.mult, ALU.add)
                P.drain("vector")
                P.op("vector", "tensor_scalar", sb["t3"].ap()[:, 0:1],
                     sb["cols"].ap()[:, 8:9], -2.0, 1.0, ALU.mult, ALU.add)
                P.drain("vector")
            P.wait_pool("vector", "dA")
            split_fp32_to_bf16(sb["t3"].ap(), sb["b16a"].ap(),
                               sb["b16b"].ap(), inc=[("vV", 1)])
            P.wait("scalar", "vV", P.cnt["vV"])
            nm = "sf" if is_sf else "cf"
            P.dma("dA", tfw[nm, "hi"].ap()[ec], sb["b16a"].ap())
            P.dma("dA", tfw[nm, "lo"].ap()[ec], sb["b16b"].ap())

    for is_si in (0, 1):
        for kc in range(NBH):
            if kc == 0:
                P.drain("vector")
                P.op("vector", "tensor_scalar", sb["xnf"].ap(), sb["nn"].ap(),
                     sb["pcol"].ap(), 0.0 if is_si else 256.0, ALU.mult,
                     ALU.add)
                fold_steps(sb["xnf"].ap(), sb["atile"].ap(),
                           (65536.0, 32768.0, 16384.0, 8192.0, 4096.0,
                            2048.0, 1024.0))
            else:
                P.drain("vector")
                P.op("vector", "tensor_tensor", sb["xnf"].ap(),
                     sb["xnf"].ap(), sb["xti0"].ap(), ALU.add)
                fold_steps(sb["xnf"].ap(), sb["atile"].ap(), (1024.0,))
            P.op("vector", "engine_nop", inc=[("vV", 1)])
            P.wait("scalar", "vV", P.cnt["vV"])
            P.drain("scalar")
            P.op("scalar", "activation", sb["atile"].ap(), sb["xnf"].ap(),
                 AF.Sin, scale=(TWO_PI_N if is_si else -TWO_PI_N),
                 bias=(sb["cols"].ap()[:, 30:31] if is_si else
                       sb["cols"].ap()[:, 29:30]),
                 inc=[("sC", 1)])
            P.wait("vector", "sC", P.cnt["sC"])
            if not is_si:
                P.op("vector", "memset", sb["wcol"].ap(), 2.0 / 1024.0)
                if kc == 0:
                    P.drain("vector")
                    P.op("vector", "memset", sb["wcol"].ap()[0:1, :],
                         1.0 / 1024.0)
                P.drain("vector")
                P.op("vector", "tensor_scalar_mul", sb["atile"].ap(),
                     sb["atile"].ap(), sb["wcol"].ap())
                P.drain("vector")
            else:
                P.op("vector", "tensor_single_scalar", sb["atile"].ap(),
                     sb["atile"].ap(), 2.0 / 1024.0, ALU.mult)
                P.drain("vector")
            P.wait_pool("vector", "dA")
            split_fp32_to_bf16(sb["atile"].ap(), sb["b16c"].ap(),
                               sb["b16d"].ap(), inc=[("vV", 1)])
            P.wait("scalar", "vV", P.cnt["vV"])
            nm = "si" if is_si else "ci"
            P.dma("dA", tin[nm, "hi"].ap()[kc], sb["b16c"].ap())
            P.dma("dA", tin[nm, "lo"].ap()[kc], sb["b16d"].ap())

    P.wait_pool("vector", "dA")
    P.op("vector", "tensor_scalar", sb["xnf"].ap()[0:1, :],
         sb["nn"].ap()[0:1, :], 512.0, 256.0, ALU.mult, ALU.add)
    fold_steps(sb["xnf"].ap()[0:1, :], sb["atile"].ap()[0:1, :],
               (262144.0, 131072.0, 65536.0, 32768.0, 16384.0, 8192.0,
                4096.0, 2048.0, 1024.0))
    P.op("vector", "engine_nop", inc=[("vV", 1)])
    P.wait("scalar", "vV", P.cnt["vV"])
    P.drain("scalar")
    P.op("scalar", "activation", sb["atile"].ap()[0:1, :],
         sb["xnf"].ap()[0:1, :], AF.Sin, scale=-TWO_PI_N,
         bias=sb["cols"].ap()[0:1, 29:30], inc=[("sC", 1)])
    P.wait("vector", "sC", P.cnt["sC"])
    P.op("vector", "tensor_single_scalar", sb["atile"].ap()[0:1, :],
         sb["atile"].ap()[0:1, :], 1.0 / 1024.0, ALU.mult)
    P.drain("vector")
    split_fp32_to_bf16(sb["atile"].ap()[0:1, :], sb["b16c"].ap()[0:1, :],
                       sb["b16d"].ap()[0:1, :], inc=[("vV", 1)])
    P.wait("scalar", "vV", P.cnt["vV"])
    P.dma("dA", tin["si", "hi"].ap()[0][0:1, :], sb["b16c"].ap()[0:1, :])
    P.dma("dA", tin["si", "lo"].ap()[0][0:1, :], sb["b16d"].ap()[0:1, :])

    toks_scr = [(f"dA{i}", P.cnt[f"dA{i}"]) for i in range(POOLS["dA"][1])
                if P.cnt[f"dA{i}"]]

    for t in toks_scr:
        P.wait_tok("sync", t)
    toks_tbl_sb = []
    for nm, half, dst in (("cf", "hi", "cfh"), ("cf", "lo", "cfl"),
                          ("sf", "hi", "sfh"), ("sf", "lo", "sfl")):
        toks_tbl_sb.append(P.dma(
            "dS", sb[dst].ap().rearrange("p (ec kb) -> p ec kb", ec=ND),
            tfw[nm, half].ap().rearrange("ec p kb -> p ec kb")))
    toks_prep = emit_fold_prep(toks_tbl_sb)

    emit_P1(0)

    slot_free = [0, 0]
    slot_loaded = [None, None]

    def load_qkv_slot(j, nm, bc):
        P.wait("sync", "tM", slot_free[j])
        t1_ = P.dma("dS", sb[f"wsl{j}h"].ap(), asp[nm, "hi"].ap()[bc])
        t2_ = P.dma("dS", sb[f"wsl{j}l"].ap(), asp[nm, "lo"].ap()[bc])
        slot_loaded[j] = (t1_, t2_)

    for tc in range(NTC):
        tt0 = tc * NTT
        if tc > 0:
            P.wait("sync", "tM", P.cnt["tM"])
            slot_free[0] = max(slot_free[0], P.cnt["tM"])
            slot_free[1] = max(slot_free[1], P.cnt["tM"])
            load_qkv_slot(0, "k", 0)
            load_qkv_slot(1, "k", 1)
            preloaded_k = True

        if tc == 0:
            for t in toks_prep:
                P.wait_tok("sync", t)
            preloaded_k = False

        def emit_P4_chunk(jb):
            P.wait("vector", "sC", P.cnt["sC"])
            if True:
                qr = sb["fq"].ap()[:, jb * TC:(jb + 1) * TC]
                qi = sb["fq"].ap()[:, (NBH + jb) * TC:(NBH + jb + 1) * TC]
                mr = sb["Mpl"].ap()[:, jb * TC:(jb + 1) * TC]
                mi = sb["Mpl"].ap()[:, (NBH + jb) * TC:(NBH + jb + 1) * TC]
                t1, t2 = sb["t1"].ap()[:, 0:TC], sb["t2"].ap()[:, 0:TC]
                t3, t4 = sb["t3"].ap()[:, 0:TC], sb["t4"].ap()[:, 0:TC]
                P.drain("vector")
                P.op("vector", "tensor_tensor", t1, mr, qr, ALU.mult)
                P.op("vector", "tensor_tensor", t2, mi, qi, ALU.mult)
                P.op("vector", "tensor_tensor", t3, mi, qr, ALU.mult)
                P.op("vector", "tensor_tensor", t4, mr, qi, ALU.mult)
                P.drain("vector")
                P.op("vector", "tensor_tensor", t1, t1, t2, ALU.add)
                P.op("vector", "tensor_tensor", t3, t3, t4, ALU.subtract)
                P.drain("vector")
                if jb == 0:
                    P.op("vector", "tensor_tensor", t1[0:1, :], mr[0:1, :],
                         qr[0:1, :], ALU.mult)
                    P.op("vector", "tensor_tensor", t3[0:1, :], mi[0:1, :],
                         qi[0:1, :], ALU.mult)
                    P.drain("vector")
                split_fp32_to_bf16(t1, sb["zh"].ap()[:, jb * TC:(jb + 1) * TC],
                                   sb["zl"].ap()[:, jb * TC:(jb + 1) * TC])
                P.drain("vector")
                split_fp32_to_bf16(
                    t3, sb["zh"].ap()[:, (NBH + jb) * TC:(NBH + jb + 1) * TC],
                    sb["zl"].ap()[:, (NBH + jb) * TC:(NBH + jb + 1) * TC],
                    inc=[("vV", 1)])

        def emit_P3():
            P.wait("vector", "sC", P.cnt["sC"])
            for jb in range(NBH):
                kr = sb["fk"].ap()[:, jb * TC:(jb + 1) * TC]
                ki = sb["fk"].ap()[:, (NBH + jb) * TC:(NBH + jb + 1) * TC]
                vr = sb["fv"].ap()[:, jb * TC:(jb + 1) * TC]
                vi = sb["fv"].ap()[:, (NBH + jb) * TC:(NBH + jb + 1) * TC]
                mr = sb["Mpl"].ap()[:, jb * TC:(jb + 1) * TC]
                mi = sb["Mpl"].ap()[:, (NBH + jb) * TC:(NBH + jb + 1) * TC]
                t1, t2 = sb["t1"].ap()[:, 0:TC], sb["t2"].ap()[:, 0:TC]
                t3, t4 = sb["t3"].ap()[:, 0:TC], sb["t4"].ap()[:, 0:TC]
                P.drain("vector")
                P.op("vector", "tensor_tensor", t1, kr, vr, ALU.mult)
                P.op("vector", "tensor_tensor", t2, ki, vi, ALU.mult)
                P.op("vector", "tensor_tensor", t3, kr, vi, ALU.mult)
                P.op("vector", "tensor_tensor", t4, ki, vr, ALU.mult)
                P.drain("vector")
                P.op("vector", "tensor_tensor", t1, t1, t2, ALU.subtract)
                P.op("vector", "tensor_tensor", t3, t3, t4, ALU.add)
                P.drain("vector")
                if jb == 0:
                    P.op("vector", "tensor_tensor", t1[0:1, :], kr[0:1, :],
                         vr[0:1, :], ALU.mult)
                    P.op("vector", "tensor_tensor", t3[0:1, :], ki[0:1, :],
                         vi[0:1, :], ALU.mult)
                    P.drain("vector")
                P.op("vector", "tensor_tensor_scan", mr,
                     sb["zeros"].ap()[:, 0:TC], t1,
                     sb["carry"].ap()[:, jb:jb + 1], ALU.mult, ALU.add)
                P.op("vector", "tensor_tensor_scan", mi,
                     sb["zeros"].ap()[:, 0:TC], t3,
                     sb["carry"].ap()[:, NBH + jb:NBH + jb + 1], ALU.mult,
                     ALU.add)
                P.drain("vector")
                P.op("vector", "tensor_copy", sb["carry"].ap()[:, jb:jb + 1],
                     mr[:, TC - 1:TC])
                P.op("vector", "tensor_copy",
                     sb["carry"].ap()[:, NBH + jb:NBH + jb + 1], mi[:, TC - 1:TC])
                P.drain("vector")


        for nm, fpl in (("k", "fk"), ("v", "fv"), ("q", "fq")):
            if nm == "q":
                emit_P3()
            for bc_i in range(2 * NBH):
                j = bc_i % 2
                if not (preloaded_k and nm == "k" and bc_i < 2):
                    load_qkv_slot(j, nm, bc_i)
                for t_ in slot_loaded[j]:
                    P.wait_tok("tensor", t_)
                bank = next_bank()
                pieces = []
                for dc in range(ND):
                    lh = sb[f"wsl{j}h"].ap()[:, dc * 128:(dc + 1) * 128]
                    ll = sb[f"wsl{j}l"].ap()[:, dc * 128:(dc + 1) * 128]
                    xh = sb["xnhT"].ap()[:, dc * TC:(dc + 1) * TC]
                    xl = sb["xnlT"].ap()[:, dc * TC:(dc + 1) * TC]
                    pieces += [(lh, xh), (lh, xl), (ll, xh)]
                tm = matmul_group(bank, pieces)
                slot_free[j] = tm
                P.wait("scalar", "tM", tm)
                P.drain("scalar")
                P.op("scalar", "copy",
                     sb[fpl].ap()[:, bc_i * TC:(bc_i + 1) * TC],
                     ps[bank].ap()[:, 0:TC], inc=[("sC", 1)])
                bank_evac_done(bank, "sC")
                if nm == "q" and bc_i >= NBH:
                    emit_P4_chunk(bc_i - NBH)

        vv_z = P.cnt["vV"]

        P.wait("sync", "vV", vv_z)
        P.wait("sync", "tM", P.cnt["tM"])
        itbl_toks = []
        for nm, half, dst in (("ci", "hi", "cih"), ("ci", "lo", "cil"),
                              ("si", "hi", "sih"), ("si", "lo", "sil")):
            itbl_toks.append(P.dma(
                "dS", sb[dst].ap().rearrange("p (kc n) -> p kc n", kc=NBH),
                tin[nm, half].ap().rearrange("kc p n -> p kc n")))
        for t in itbl_toks:
            P.wait_tok("tensor", t)
        P.wait("tensor", "vV", vv_z)
        spill_toks = []
        for tt in range(NTT):
            for nh in range(2):
                bank = next_bank()
                pieces = []
                for bc_i in range(2 * NBH):
                    is_im = bc_i >= NBH
                    jb = bc_i % NBH
                    zh = sb["zh"].ap()[:, bc_i * TC + tt * 128:
                                       bc_i * TC + (tt + 1) * 128]
                    zl = sb["zl"].ap()[:, bc_i * TC + tt * 128:
                                       bc_i * TC + (tt + 1) * 128]
                    th = sb["sih" if is_im else "cih"].ap()[
                        :, jb * D + nh * 512:jb * D + (nh + 1) * 512]
                    tl = sb["sil" if is_im else "cil"].ap()[
                        :, jb * D + nh * 512:jb * D + (nh + 1) * 512]
                    pieces += [(zh, th), (zh, tl), (zl, th)]
                tm = matmul_group(bank, pieces)
                P.wait("scalar", "tM", tm)
                P.wait_pool("scalar", "dA")
                P.drain("scalar")
                P.op("scalar", "copy",
                     sb["atile"].ap()[:, nh * 512:(nh + 1) * 512],
                     ps[bank].ap()[:, 0:512], inc=[("sC", 1)])
                bank_evac_done(bank, "sC")
            P.drain("scalar")
            spill_toks.append(P.dma("dA", attn_t[tt0 + tt], sb["atile"].ap()))
        tm_p5 = P.cnt["tM"]
        slot_free[1] = max(slot_free[1], tm_p5)

        pairs = ((sb["fsl1"].ap(), sb["fsl2"].ap()),
                 (sb["fsl0"].ap()[:, 0:D], sb["fsl0"].ap()[:, D:2 * D]))
        pair_free = [slot_free[0], slot_free[1]]
        for oc in range(NO):
            pj = oc % 2
            sl1, sl2 = pairs[pj]
            P.wait("gpsimd", "tM", pair_free[pj])
            dv1 = P.dma("dG", sl1, w1t.ap()[oc])
            dv2 = P.dma("dG", sl2, w2t.ap()[oc])
            P.wait_tok("tensor", dv1)
            P.wait_tok("tensor", dv2)
            res = []
            for wsl in (sl1, sl2):
                bank = next_bank()
                pieces = []
                for dc in range(ND):
                    wblk = wsl[:, dc * 128:(dc + 1) * 128]
                    xh = sb["xnhT"].ap()[:, dc * TC:(dc + 1) * TC]
                    xl = sb["xnlT"].ap()[:, dc * TC:(dc + 1) * TC]
                    pieces += [(wblk, xh), (wblk, xl)]
                res.append((bank, matmul_group(bank, pieces)))
            pair_free[pj] = res[-1][1]
            (b1, tm1), (b2, tm2) = res
            P.wait("scalar", "tM", tm1)
            P.wait("scalar", "vV", P.cnt["vV"])
            P.op("scalar", "activation", sb["t1"].ap()[:, 0:TC],
                 ps[b1].ap()[:, 0:TC], AF.Silu,
                 scale=sb["scl"].ap()[:, oc:oc + 1])
            P.drain("scalar")
            P.wait("scalar", "tM", tm2)
            P.op("scalar", "activation", sb["t2"].ap()[:, 0:TC],
                 ps[b2].ap()[:, 0:TC], AF.Identity,
                 scale=sb["scl"].ap()[:, NO + oc:NO + oc + 1],
                 inc=[("sC", 1)])
            P.wait("vector", "sC", P.cnt["sC"])
            P.drain("vector")
            P.op("vector", "tensor_tensor", sb["t3"].ap()[:, 0:TC],
                 sb["t1"].ap()[:, 0:TC], sb["t2"].ap()[:, 0:TC], ALU.mult)
            P.drain("vector")
            split_fp32_to_bf16(sb["t3"].ap()[:, 0:TC],
                               sb["arA_h"].ap()[:, oc * TC:(oc + 1) * TC],
                               sb["arA_l"].ap()[:, oc * TC:(oc + 1) * TC],
                               inc=[("vV", 1)])
            bank_evac_done(b1)
            bank_evac_done(b2)

        tm_after_p6 = P.cnt["tM"]

        half_free = [max(slot_free[1], pair_free[1]),
                     max(slot_free[1], pair_free[1])]
        for hb, src_h, src_l, dst_h, dst_l, scoff in (
                (h1t, "arA_h", "arA_l", "arB_h", "arB_l", 2 * NO),
                (h2t, "arB_h", "arB_l", "arA_h", "arA_l", 3 * NO)):
            vv_src = P.cnt["vV"]
            for oc in range(NO):
                if hb is h1t and oc == 10 and tc + 1 < NTC:
                    emit_P1(tc + 1, overlap=True)
                dvh = []
                for h in range(2):
                    P.wait("gpsimd", "tM", half_free[h])
                    dvh.append(P.dma(
                        "dG", sb["fsl0"].ap()[:, h * 2048:(h + 1) * 2048],
                        hb.ap()[oc][:, h * 2048:(h + 1) * 2048]))
                bank = next_bank()
                P.wait("tensor", *ps_free[bank])
                P.wait("tensor", "vV", vv_src)
                for h in range(2):
                    P.wait_tok("tensor", dvh[h])
                    for kci in range(16):
                        kc = h * 16 + kci
                        wblk = sb["fsl0"].ap()[:, kc * 128:(kc + 1) * 128]
                        ah = sb[src_h].ap()[:, kc * TC:(kc + 1) * TC]
                        al = sb[src_l].ap()[:, kc * TC:(kc + 1) * TC]
                        P.op("tensor", "matmul", ps[bank].ap()[:, 0:TC],
                             wblk, ah, start=(kc == 0), stop=False)
                        lastall = kc == NO - 1
                        P.op("tensor", "matmul", ps[bank].ap()[:, 0:TC],
                             wblk, al, start=False, stop=lastall,
                             inc=[("tM", 1)] if kci == 15 else ())
                    half_free[h] = P.cnt["tM"]
                tm = P.cnt["tM"]
                P.wait("scalar", "tM", tm)
                P.wait("scalar", "vV", P.cnt["vV"])
                P.op("scalar", "activation", sb["t1"].ap()[:, 0:TC],
                     ps[bank].ap()[:, 0:TC], AF.Gelu,
                     scale=sb["scl"].ap()[:, scoff + oc:scoff + oc + 1],
                     inc=[("sC", 1)])
                P.wait("vector", "sC", P.cnt["sC"])
                split_fp32_to_bf16(sb["t1"].ap()[:, 0:TC],
                                   sb[dst_h].ap()[:, oc * TC:(oc + 1) * TC],
                                   sb[dst_l].ap()[:, oc * TC:(oc + 1) * TC],
                                   inc=[("vV", 1)])
                bank_evac_done(bank)

        slot_free[0] = pair_free[0]
        slot_free[1] = max(half_free)
        tm_h2_done = P.cnt["tM"]
        P.wait("gpsimd", "tM", tm_h2_done)
        w3_toks = []
        for nh in range(2):
            for q4 in range(4):
                off = nh * NO * 512 + q4 * NO * 128
                w3_toks.append(P.dma(
                    "dG", sb["w3pre"].ap()[:, off:off + NO * 128],
                    w3t.ap()[nh][:, q4 * NO * 128:(q4 + 1) * NO * 128]))
        vv_c = P.cnt["vV"]
        for t in w3_toks:
            P.wait_tok("tensor", t)

        pf_bufs = ((sb["xpf0"], sb["apf0"]), (sb["xti0"], sb["xti1"]))

        def pf_load(tt):
            xb, ab = pf_bufs[tt % 2]
            P.wait("sync", "vV", P.cnt["vV"])
            tx = P.dma("dS", xb.ap(), x_t[tt0 + tt])
            P.wait_tok("sync", spill_toks[tt])
            ta = P.dma("dS", ab.ap(), attn_t[tt0 + tt])
            return (tx, ta)

        pf_toks = [pf_load(0)]
        for tt in range(NTT):
            banks = [next_bank(), next_bank()]
            P.wait("tensor", "vV", vv_c)
            for nh in range(2):
                P.wait("tensor", *ps_free[banks[nh]])
            for kc in range(NO):
                ch = sb["arA_h"].ap()[:, kc * TC + tt * 128:
                                      kc * TC + (tt + 1) * 128]
                cl = sb["arA_l"].ap()[:, kc * TC + tt * 128:
                                      kc * TC + (tt + 1) * 128]
                for nh in range(2):
                    w = sb["w3pre"].ap()[:, (nh * NO + kc) * 512:
                                         (nh * NO + kc + 1) * 512]
                    last = kc == NO - 1
                    P.op("tensor", "matmul", ps[banks[nh]].ap()[:, 0:512],
                         ch, w, start=(kc == 0), stop=False)
                    P.op("tensor", "matmul", ps[banks[nh]].ap()[:, 0:512],
                         cl, w, start=False, stop=last,
                         inc=[("tM", 1)] if last and nh == 1 else ())
            tm_tt = P.cnt["tM"]
            if tt + 1 < NTT:
                pf_toks.append(pf_load(tt + 1))
            xb, ab = pf_bufs[tt % 2]
            tx, ta = pf_toks[tt]
            P.wait("vector", "tM", tm_tt)
            P.wait_tok("vector", tx)
            P.wait_tok("vector", ta)
            P.wait_pool("vector", "dA")
            for nh in range(2):
                seg = slice(nh * 512, (nh + 1) * 512)
                P.drain("vector")
                P.op("vector", "tensor_tensor", sb["t3"].ap()[:, 0:512],
                     ps[banks[nh]].ap()[:, 0:512], sb["s3bc"].ap()[:, seg],
                     ALU.mult)
                P.drain("vector")
                P.op("vector", "tensor_tensor", sb["t3"].ap()[:, 0:512],
                     sb["t3"].ap()[:, 0:512], ab.ap()[:, seg], ALU.add)
                P.drain("vector")
                P.op("vector", "tensor_tensor", sb["atile"].ap()[:, seg],
                     sb["t3"].ap()[:, 0:512], xb.ap()[:, seg],
                     ALU.add, inc=[("vV", 1)])
                bank_evac_done(banks[nh])
            P.wait("scalar", "vV", P.cnt["vV"])
            P.dma("dA", y_t[tt0 + tt], sb["atile"].ap())

    P.wait_pool("scalar", "dA")
    P.wait_pool("scalar", "dS")
    P.wait_pool("scalar", "dG")

    with nc.Block() as block:
        @block.tensor
        def _(e):
            for f in P.q["tensor"]:
                f(e)

        @block.vector
        def _(e):
            for f in P.q["vector"]:
                f(e)

        @block.scalar
        def _(e):
            for f in P.q["scalar"]:
                f(e)

        @block.sync
        def _(e):
            for f in P.q["sync"]:
                f(e)

        @block.gpsimd
        def _(e):
            for f in P.q["gpsimd"]:
                f(e)

    ctx.close()
    return nc


def host_layouts(inputs, T_TOT):
    f = np.float32

    def stat4(w):
        no_, ni = w.shape
        noc, nic = no_ // 128, ni // 128
        return np.ascontiguousarray(
            w.reshape(noc, 128, nic, 128).transpose(0, 3, 2, 1)).reshape(
            noc, 128, ni)

    com = {
        "wqT": np.ascontiguousarray(
            np.asarray(inputs["L0_attn_Wq"], f).reshape(ND, 128, D)),
        "wkT": np.ascontiguousarray(
            np.asarray(inputs["L0_attn_Wk"], f).reshape(ND, 128, D)),
        "wvT": np.ascontiguousarray(
            np.asarray(inputs["L0_attn_Wv"], f).reshape(ND, 128, D)),
        "w1t": stat4(np.asarray(inputs["L0_ffn_w1_w"], f)),
        "w2t": stat4(np.asarray(inputs["L0_ffn_w2_w"], f)),
        "h1t": stat4(np.asarray(inputs["L0_ffn_h1_w"], f)),
        "h2t": stat4(np.asarray(inputs["L0_ffn_h2_w"], f)),
        "w3t": np.ascontiguousarray(
            np.asarray(inputs["L0_ffn_w3_w"], f).T.reshape(NO, 128, 2, 512)
            .transpose(2, 1, 0, 3)).reshape(2, 128, NO * 512),
        "s1_in": np.ascontiguousarray(
            np.asarray(inputs["L0_ffn_w1_s"], f).reshape(NO, 128).T),
        "s2_in": np.ascontiguousarray(
            np.asarray(inputs["L0_ffn_w2_s"], f).reshape(NO, 128).T),
        "sh1_in": np.ascontiguousarray(
            np.asarray(inputs["L0_ffn_h1_s"], f).reshape(NO, 128).T),
        "sh2_in": np.ascontiguousarray(
            np.asarray(inputs["L0_ffn_h2_s"], f).reshape(NO, 128).T),
        "s3_in": np.ascontiguousarray(
            np.asarray(inputs["L0_ffn_w3_s"], f).reshape(1, D)),
    }
    x = np.asarray(inputs["L0_attn_x"], f)
    per_core = []
    for b in range(x.shape[0]):
        m = dict(com)
        m["x_in"] = np.ascontiguousarray(x[b, :T_TOT, :])
        per_core.append(m)
    return per_core


def kernel(**inputs):
    T_TOT, TC = 2048, 512
    nc = build_nc(T_TOT, TC)
    in_maps = host_layouts(inputs, T_TOT)
    res = run_bass_kernel_spmd(nc, in_maps, core_ids=list(range(8)))
    out = np.stack([np.asarray(res.results[i]["y"]) for i in range(8)], axis=0)
    return out.astype(np.float32)


# version 18
# speedup vs baseline: 1.0788x; 1.0788x over previous
import contextlib

import numpy as np

import concourse.bass as bass
import concourse.mybir as mybir
from concourse.bass_utils import run_bass_kernel_spmd

F32 = mybir.dt.float32
BF16 = mybir.dt.bfloat16
U8 = mybir.dt.uint8
AF = mybir.ActivationFunctionType
ALU = mybir.AluOpType
AX = mybir.AxisListType

D = 1024
HID = 4096
ND = D // 128
NO = HID // 128
NBH = 4
TWO_PI_N = float(2.0 * np.pi / 1024.0)

ENGS = ("tensor", "vector", "scalar", "sync", "gpsimd")
POOLS = {"dS": ("sync", 6), "dA": ("scalar", 4), "dG": ("gpsimd", 4)}


class Plan:

    def __init__(self, sems):
        self.q = {e: [] for e in ENGS}
        self.sems = sems
        self.cnt = {s: 0 for s in sems}
        self.pool_idx = {p: 0 for p in POOLS}

    def op(self, eng, meth, *args, inc=(), **kw):
        inc = tuple(inc)

        def f(e, meth=meth, args=args, kw=kw, inc=inc):
            ins = getattr(e, meth)(*args, **kw)
            for s, d in inc:
                ins = ins.then_inc(self.sems[s], d)

        self.q[eng].append(f)
        for s, d in inc:
            self.cnt[s] += d

    def dma(self, pool, out_ap, in_ap, **kw):
        eng, n = POOLS[pool]
        s = f"{pool}{self.pool_idx[pool] % n}"
        self.pool_idx[pool] += 1
        self.wait(eng, s, self.cnt[s])
        self.op(eng, "dma_start", out_ap, in_ap, inc=[(s, 16)], **kw)
        return (s, self.cnt[s])

    def wait(self, eng, sem, val):
        def f(e, sem=sem, val=val):
            e.wait_ge(self.sems[sem], val)

        self.q[eng].append(f)

    def wait_tok(self, eng, tok):
        self.wait(eng, tok[0], tok[1])

    def wait_pool(self, eng, pool):
        _, n = POOLS[pool]
        for i in range(n):
            s = f"{pool}{i}"
            if self.cnt[s]:
                self.wait(eng, s, self.cnt[s])

    def drain(self, eng):
        self.q[eng].append(lambda e: e.drain())


def build_nc(T_TOT, TC):
    NTC = T_TOT // TC
    NTT = TC // 128

    nc = bass.Bass("TRN2", target_bir_lowering=False)

    x_in = nc.dram_tensor("x_in", [T_TOT, D], F32, kind="ExternalInput")
    wqT = nc.dram_tensor("wqT", [ND, 128, D], F32, kind="ExternalInput")
    wkT = nc.dram_tensor("wkT", [ND, 128, D], F32, kind="ExternalInput")
    wvT = nc.dram_tensor("wvT", [ND, 128, D], F32, kind="ExternalInput")
    w1t = nc.dram_tensor("w1t", [NO, 128, ND * 128], F32, kind="ExternalInput")
    w2t = nc.dram_tensor("w2t", [NO, 128, ND * 128], F32, kind="ExternalInput")
    h1t = nc.dram_tensor("h1t", [NO, 128, NO * 128], F32, kind="ExternalInput")
    h2t = nc.dram_tensor("h2t", [NO, 128, NO * 128], F32, kind="ExternalInput")
    w3t = nc.dram_tensor("w3t", [2, 128, NO * 512], F32,
                         kind="ExternalInput")
    s1_in = nc.dram_tensor("s1_in", [128, NO], F32,
                          kind="ExternalInput")
    s2_in = nc.dram_tensor("s2_in", [128, NO], F32,
                          kind="ExternalInput")
    sh1_in = nc.dram_tensor("sh1_in", [128, NO], F32,
                          kind="ExternalInput")
    sh2_in = nc.dram_tensor("sh2_in", [128, NO], F32,
                          kind="ExternalInput")
    s3_in = nc.dram_tensor("s3_in", [1, D], F32, kind="ExternalInput")

    y_out = nc.dram_tensor("y", [T_TOT, D], F32, kind="ExternalOutput")

    asp = {}
    for nm in ("q", "k", "v"):
        for half in ("hi", "lo"):
            asp[nm, half] = nc.dram_tensor(f"a{nm}_{half}",
                                           [2 * NBH, 128, D], BF16,
                                           kind="Internal")
    tfw = {}
    for nm in ("cf", "sf"):
        for half in ("hi", "lo"):
            tfw[nm, half] = nc.dram_tensor(f"{nm}_{half}", [ND, 128, 512],
                                           BF16, kind="Internal")
    tin = {}
    for nm in ("ci", "si"):
        for half in ("hi", "lo"):
            tin[nm, half] = nc.dram_tensor(f"{nm}_{half}", [NBH, 128, D],
                                           BF16, kind="Internal")
    attn_s = nc.dram_tensor("attn_s", [T_TOT, D], F32, kind="Internal")

    x_t = x_in.ap().rearrange("(tt p) d -> tt p d", p=128)
    y_t = y_out.ap().rearrange("(tt p) d -> tt p d", p=128)
    attn_t = attn_s.ap().rearrange("(tt p) d -> tt p d", p=128)

    ctx = contextlib.ExitStack()

    sb = {}

    def sb_alloc(name, shape, dtype):
        sb[name] = ctx.enter_context(nc.sbuf_tensor(name, shape, dtype))
        return sb[name]

    sb_alloc("idn", [128, 128], BF16)
    sb_alloc("xti0", [128, D], F32)
    sb_alloc("xti1", [128, D], F32)
    sb_alloc("xnf", [128, D], F32)
    sb_alloc("xh_tok", [128, D], BF16)
    sb_alloc("xl_tok", [128, D], BF16)
    sb_alloc("xnhT", [128, ND * TC], BF16)
    sb_alloc("xnlT", [128, ND * TC], BF16)
    sb_alloc("t1", [128, 512], F32)
    sb_alloc("t2", [128, 512], F32)
    sb_alloc("t3", [128, 512], F32)
    sb_alloc("t4", [128, 512], F32)
    sb_alloc("atile", [128, D], F32)
    sb_alloc("zeros", [128, 512], F32)
    sb_alloc("carry", [128, 8], F32)
    sb_alloc("cols", [128, 32], F32)
    sb_alloc("scl", [128, 4 * NO], F32)
    sb_alloc("s3bc", [128, D], F32)
    sb_alloc("kk", [128, 512], F32)
    sb_alloc("nn", [128, D], F32)
    sb_alloc("ncol", [128, 1], F32)
    sb_alloc("pcol", [128, 1], F32)
    sb_alloc("wcol", [128, 1], F32)
    sb_alloc("b16a", [128, 512], BF16)
    sb_alloc("b16b", [128, 512], BF16)
    sb_alloc("b16c", [128, D], BF16)
    sb_alloc("b16d", [128, D], BF16)

    POOLB = 140 * 1024
    pool_h = ctx.enter_context(nc.sbuf_tensor("pool", [128, POOLB], U8))
    pool_base = nc.lookup_mloc(pool_h).addr

    def pool_at(name, off_kib, shape, dtype):
        h = nc.virtual_sbuf_tensor(name, shape, dtype)
        nc._tensor(h.name, shape, dtype, type="SB")
        mloc = nc.lookup_mloc(h)
        mloc.allocated = True
        mloc.addr = pool_base + off_kib * 1024
        sb[name] = h
        return h

    pool_at("uh", 0, [128, ND * TC], BF16)
    pool_at("ul", 8, [128, ND * TC], BF16)
    pool_at("cfh", 16, [128, ND * 512], BF16)
    pool_at("cfl", 24, [128, ND * 512], BF16)
    pool_at("sfh", 32, [128, ND * 512], BF16)
    pool_at("sfl", 40, [128, ND * 512], BF16)
    pool_at("wsl0h", 48, [128, D], BF16)
    pool_at("wsl0l", 50, [128, D], BF16)
    pool_at("wsl1h", 52, [128, D], BF16)
    pool_at("wsl1l", 54, [128, D], BF16)
    pool_at("fq", 56, [128, 2 * NBH * TC], F32)
    pool_at("fk", 72, [128, 2 * NBH * TC], F32)
    pool_at("fv", 88, [128, 2 * NBH * TC], F32)
    pool_at("Mpl", 104, [128, 2 * NBH * TC], F32)
    pool_at("zh", 120, [128, 2 * NBH * TC], BF16)
    pool_at("zl", 128, [128, 2 * NBH * TC], BF16)
    pool_at("wfh", 56, [128, ND * D], BF16)
    pool_at("wfl", 72, [128, ND * D], BF16)
    pool_at("wgh", 88, [128, ND * D], BF16)
    pool_at("wgl", 104, [128, ND * D], BF16)
    pool_at("cih", 56, [128, NBH * D], BF16)
    pool_at("cil", 64, [128, NBH * D], BF16)
    pool_at("sih", 72, [128, NBH * D], BF16)
    pool_at("sil", 80, [128, NBH * D], BF16)
    pool_at("arA_h", 0, [128, NO * TC], BF16)
    pool_at("arA_l", 32, [128, NO * TC], BF16)
    pool_at("arB_h", 64, [128, NO * TC], BF16)
    pool_at("arB_l", 96, [128, NO * TC], BF16)
    pool_at("w3pre", 64, [128, 2 * NO * 512], BF16)
    pool_at("xpf0", 128, [128, D], F32)
    pool_at("apf0", 132, [128, D], F32)
    pool_at("fsl0", 128, [128, NO * 128], BF16)
    pool_at("fsl1", 136, [128, D], BF16)
    pool_at("fsl2", 138, [128, D], BF16)

    ps = [ctx.enter_context(nc.psum_tensor(f"ps{i}", [128, 512], F32))
          for i in range(6)]
    pstA = ctx.enter_context(nc.psum_tensor("pstA", [128, 1024], BF16))
    pstB = ctx.enter_context(nc.psum_tensor("pstB", [128, 1024], BF16))

    semnames = ["tM", "vV", "sC", "gP"]
    for p, (_, np_) in POOLS.items():
        semnames += [f"{p}{i}" for i in range(np_)]
    sems = {s: ctx.enter_context(nc.semaphore(s)) for s in semnames}
    P = Plan(sems)

    def split_fp32_to_bf16(src_ap, hi_ap, lo_ap, inc=()):
        P.op("vector", "tensor_copy", hi_ap, src_ap)
        P.drain("vector")
        P.op("vector", "tensor_tensor", lo_ap, src_ap, hi_ap, ALU.subtract,
             inc=inc)

    ps_free = [("vV", 0)] * 6

    def matmul_group(bank, pieces):
        P.wait("tensor", *ps_free[bank])
        width = pieces[0][1].shape[-1]
        n = len(pieces)
        for i, (l, r) in enumerate(pieces):
            P.op("tensor", "matmul", ps[bank].ap()[:, 0:width], l, r,
                 start=(i == 0), stop=(i == n - 1),
                 inc=[("tM", 1)] if i == n - 1 else ())
        return P.cnt["tM"]

    ps_rr = [0]

    def next_bank():
        i = ps_rr[0] % 6
        ps_rr[0] += 1
        return i

    def bank_evac_done(bank, sem="vV"):
        ps_free[bank] = (sem, P.cnt[sem])

    P.op("gpsimd", "iota", sb["t1"].ap()[:, 0:128], [[1, 128]], base=0,
         channel_multiplier=-1, allow_small_or_imprecise_dtypes=True)
    P.op("gpsimd", "iota", sb["kk"].ap(), [[1, 512]], base=0,
         channel_multiplier=0, allow_small_or_imprecise_dtypes=True)
    P.op("gpsimd", "iota", sb["nn"].ap(), [[1, D]], base=0,
         channel_multiplier=0, allow_small_or_imprecise_dtypes=True)
    P.op("gpsimd", "iota", sb["pcol"].ap(), [[0, 1]], base=0,
         channel_multiplier=1, allow_small_or_imprecise_dtypes=True,
         inc=[("gP", 1)])
    P.wait("vector", "gP", P.cnt["gP"])
    P.op("vector", "tensor_single_scalar", sb["idn"].ap(),
         sb["t1"].ap()[:, 0:128], 0.0, ALU.is_equal)
    P.op("vector", "memset", sb["carry"].ap(), 0.0)
    P.op("vector", "memset", sb["cols"].ap()[:, 28:29], 1e-5)
    P.op("vector", "memset", sb["cols"].ap()[:, 29:30], float(np.pi))
    P.op("vector", "memset", sb["cols"].ap()[:, 30:31], -float(np.pi))
    P.op("vector", "memset", sb["zeros"].ap(), 1.0, inc=[("vV", 1)])

    for i, src in enumerate((s1_in, s2_in, sh1_in, sh2_in)):
        P.dma("dS", sb["scl"].ap()[:, i * NO:(i + 1) * NO], src.ap())
    dv = P.dma("dS", sb["s3bc"].ap()[0:1, :], s3_in.ap())
    n = 1
    while n < 128:
        P.wait_tok("sync", dv)
        dv = P.dma("dS", sb["s3bc"].ap()[n:2 * n, :], sb["s3bc"].ap()[0:n, :])
        n *= 2

    def emit_P1_stats(tc, overlap=False):
        tt0 = tc * NTT
        if overlap:
            xb_tiles = tuple(sb["xti0"] if t % 2 == 0 else sb["xti1"]
                             for t in range(NTT))
        else:
            xb_tiles = (sb["xti0"], sb["xti1"], sb["xpf0"], sb["apf0"])
        dvs = [None] * NTT
        for tt in range(NTT):
            xbuf = xb_tiles[tt]
            if (not overlap) or tt < 2:
                P.wait("sync", "vV", P.cnt["vV"])
                P.wait("sync", "sC", P.cnt["sC"])
                dvs[tt] = P.dma("dS", xbuf.ap(), x_t[tt0 + tt])
        for tt in range(NTT):
            xbuf = xb_tiles[tt]
            if dvs[tt] is None:
                P.wait("sync", "vV", P.cnt["vV"])
                P.wait("sync", "sC", P.cnt["sC"])
                dvs[tt] = P.dma("dS", xbuf.ap(), x_t[tt0 + tt])
            P.wait_tok("vector", dvs[tt])
            P.drain("vector")
            P.op("vector", "reduce_sum", sb["cols"].ap()[:, tt:tt + 1],
                 xbuf.ap(), axis=AX.X, inc=[("vV", 1)])
            P.wait_tok("scalar", dvs[tt])
            if tt == 0:
                P.wait("scalar", "vV", P.cnt["vV"] - 1)
            P.drain("scalar")
            P.op("scalar", "activation", sb["xnf"].ap(), xbuf.ap(),
                 AF.Square, accum_out=sb["cols"].ap()[:, 4 + tt:5 + tt],
                 inc=[("sC", 1)])
        P.wait("vector", "sC", P.cnt["sC"])
        P.drain("vector")
        P.op("vector", "tensor_single_scalar", sb["cols"].ap()[:, 8:8 + NTT],
             sb["cols"].ap()[:, 0:NTT], 1.0 / D, ALU.mult)
        P.op("vector", "tensor_single_scalar", sb["cols"].ap()[:, 12:12 + NTT],
             sb["cols"].ap()[:, 4:4 + NTT], 1.0 / D, ALU.mult)
        P.drain("vector")
        P.op("vector", "tensor_tensor", sb["cols"].ap()[:, 16:16 + NTT],
             sb["cols"].ap()[:, 8:8 + NTT], sb["cols"].ap()[:, 8:8 + NTT],
             ALU.mult)
        P.drain("vector")
        P.op("vector", "tensor_tensor", sb["cols"].ap()[:, 12:12 + NTT],
             sb["cols"].ap()[:, 12:12 + NTT], sb["cols"].ap()[:, 16:16 + NTT],
             ALU.subtract, inc=[("vV", 1)])
        P.wait("scalar", "vV", P.cnt["vV"])
        P.drain("scalar")
        P.op("scalar", "activation", sb["cols"].ap()[:, 16:16 + NTT],
             sb["cols"].ap()[:, 12:12 + NTT], AF.Sqrt,
             bias=sb["cols"].ap()[:, 28:29], inc=[("sC", 1)])
        P.wait("vector", "sC", P.cnt["sC"])
        P.drain("vector")
        P.op("vector", "reciprocal", sb["cols"].ap()[:, 20:20 + NTT],
             sb["cols"].ap()[:, 16:16 + NTT])
        P.drain("vector")
        P.op("vector", "scalar_tensor_tensor", sb["cols"].ap()[:, 24:24 + NTT],
             sb["cols"].ap()[:, 8:8 + NTT], -1.0,
             sb["cols"].ap()[:, 20:20 + NTT], ALU.mult, ALU.mult,
             inc=[("vV", 1)])
        return xb_tiles

    def emit_P1_tile(tc, tt, xb_tiles, overlap=False):
        tt0 = tc * NTT
        xbuf = xb_tiles[tt]
        if overlap:
            P.wait("sync", "vV", P.cnt["vV"])
            P.wait("sync", "sC", P.cnt["sC"])
            rdv = P.dma("dS", xbuf.ap(), x_t[tt0 + tt])
            P.wait_tok("scalar", rdv)
            P.wait("scalar", "tM", tm_after_p6)
        P.wait("scalar", "vV", P.cnt["vV"])
        P.drain("scalar")
        P.op("scalar", "activation", sb["xnf"].ap(), xbuf.ap(),
             AF.Identity, bias=sb["cols"].ap()[:, 24 + tt:25 + tt],
             scale=sb["cols"].ap()[:, 20 + tt:21 + tt], inc=[("sC", 1)])
        P.wait("vector", "sC", P.cnt["sC"])
        P.drain("vector")
        split_fp32_to_bf16(sb["xnf"].ap(), sb["xh_tok"].ap(),
                           sb["xl_tok"].ap(), inc=[("vV", 1)])
        vv_split = P.cnt["vV"]
        P.wait("tensor", "vV", vv_split)
        sc_ev = P.cnt["sC"]
        i = 0
        for tok, dstT in ((sb["xh_tok"], sb["xnhT"]),
                          (sb["xl_tok"], sb["xnlT"])):
            for dc in range(ND):
                bankT = pstA if i % 2 == 0 else pstB
                if i >= 2:
                    P.wait("tensor", "sC", sc_ev + i - 1)
                P.op("tensor", "transpose", bankT.ap()[:, 0:128],
                     tok.ap()[:, dc * 128:(dc + 1) * 128],
                     sb["idn"].ap(), inc=[("tM", 1)])
                P.wait("scalar", "tM", P.cnt["tM"])
                P.op("scalar", "copy",
                     dstT.ap()[:, dc * TC + tt * 128:
                               dc * TC + (tt + 1) * 128],
                     bankT.ap()[:, 0:128], inc=[("sC", 1)])
                i += 1
        P.wait("tensor", "sC", P.cnt["sC"])

    def emit_P1(tc, overlap=False):
        xb = emit_P1_stats(tc, overlap)
        for tt in range(NTT):
            emit_P1_tile(tc, tt, xb, overlap)

    def emit_fold_prep(toks_tbl_sb):
        st_toks = []
        projs = (("k", wkT), ("v", wvT), ("q", wqT))
        pairs = ((sb["wfh"], sb["wfl"]), (sb["wgh"], sb["wgl"]))
        plane_free = [0, 0]
        vv_wd = {}

        def emit_wsplit(pi):
            nm, wt = projs[pi]
            ph, pl = pairs[pi % 2]
            P.wait("vector", "tM", plane_free[pi % 2])
            for ec in range(ND):
                xb = sb["xti0"] if ec % 2 == 0 else sb["xti1"]
                P.wait("sync", "vV", P.cnt["vV"])
                dv = P.dma("dS", xb.ap(), wt.ap()[ec])
                P.wait_tok("vector", dv)
                P.drain("vector")
                split_fp32_to_bf16(
                    xb.ap(), ph.ap()[:, ec * D:(ec + 1) * D],
                    pl.ap()[:, ec * D:(ec + 1) * D], inc=[("vV", 1)])
            vv_wd[pi] = P.cnt["vV"]

        emit_wsplit(0)
        for pi, (nm, wt) in enumerate(projs):
            if pi + 1 < len(projs):
                emit_wsplit(pi + 1)
            ph, pl = pairs[pi % 2]
            P.wait("tensor", "vV", vv_wd[pi])
            for t_ in toks_tbl_sb:
                P.wait_tok("tensor", t_)
            for dc in range(ND):
                for is_im in (0, 1):
                    th = sb["sfh" if is_im else "cfh"]
                    tl = sb["sfl" if is_im else "cfl"]
                    bank = next_bank()
                    pieces = []
                    for ec in range(ND):
                        wh = ph.ap()[:, ec * D + dc * 128:
                                     ec * D + (dc + 1) * 128]
                        wl = pl.ap()[:, ec * D + dc * 128:
                                     ec * D + (dc + 1) * 128]
                        tt_h = th.ap()[:, ec * 512:(ec + 1) * 512]
                        tt_l = tl.ap()[:, ec * 512:(ec + 1) * 512]
                        pieces += [(wh, tt_h), (wl, tt_h), (wh, tt_l)]
                    tm = matmul_group(bank, pieces)
                    P.wait("vector", "tM", tm)
                    for t_ in st_toks[-8:]:
                        P.wait_tok("vector", t_)
                    split_fp32_to_bf16(ps[bank].ap()[:, 0:512],
                                       sb["b16a"].ap(), sb["b16b"].ap(),
                                       inc=[("vV", 1)])
                    bank_evac_done(bank)
                    P.wait("scalar", "vV", P.cnt["vV"])
                    for bj in range(4):
                        bc = is_im * NBH + bj
                        st_toks.append(P.dma(
                            "dA",
                            asp[nm, "hi"].ap()[bc][:, dc * 128:
                                                   (dc + 1) * 128],
                            sb["b16a"].ap()[:, bj * 128:(bj + 1) * 128]))
                        st_toks.append(P.dma(
                            "dA",
                            asp[nm, "lo"].ap()[bc][:, dc * 128:
                                                   (dc + 1) * 128],
                            sb["b16b"].ap()[:, bj * 128:(bj + 1) * 128]))
            plane_free[pi % 2] = P.cnt["tM"]
        return [(f"dA{i}", P.cnt[f"dA{i}"]) for i in range(POOLS["dA"][1])
                if P.cnt[f"dA{i}"]]

    def fold_steps(tmp, mask, steps):
        for sft in steps:
            P.drain("vector")
            P.op("vector", "tensor_single_scalar", mask, tmp, sft, ALU.is_ge)
            P.drain("vector")
            P.op("vector", "scalar_tensor_tensor", tmp, mask, -sft, tmp,
                 ALU.mult, ALU.add)

    def fold_to_signed(tmp, mask):
        fold_steps(tmp, mask, (512.0,))
        P.drain("vector")
        P.op("vector", "scalar_tensor_tensor", tmp, mask, -512.0, tmp,
             ALU.mult, ALU.add)

    P.drain("vector")
    P.op("vector", "tensor_single_scalar", sb["t4"].ap(), sb["kk"].ap(),
         128.0, ALU.mult)
    fold_steps(sb["t4"].ap(), sb["t2"].ap(),
               (32768.0, 16384.0, 8192.0, 4096.0, 2048.0, 1024.0))
    P.drain("vector")
    P.op("vector", "tensor_single_scalar", sb["xti0"].ap(), sb["nn"].ap(),
         128.0, ALU.mult)
    fold_steps(sb["xti0"].ap(), sb["atile"].ap(),
               (65536.0, 32768.0, 16384.0, 8192.0, 4096.0, 2048.0, 1024.0))

    for is_sf in (0, 1):
        for ec in range(ND):
            if ec == 0:
                P.drain("vector")
                P.op("vector", "tensor_scalar", sb["t1"].ap(), sb["kk"].ap(),
                     sb["pcol"].ap(), 0.0 if is_sf else 256.0, ALU.mult,
                     ALU.add)
                fold_steps(sb["t1"].ap(), sb["t2"].ap(),
                           (65536.0, 32768.0, 16384.0, 8192.0, 4096.0,
                            2048.0, 1024.0))
            else:
                P.drain("vector")
                P.op("vector", "tensor_tensor", sb["t1"].ap(), sb["t1"].ap(),
                     sb["t4"].ap(), ALU.add)
                fold_steps(sb["t1"].ap(), sb["t2"].ap(), (1024.0,))
            P.op("vector", "engine_nop", inc=[("vV", 1)])
            P.wait("scalar", "vV", P.cnt["vV"])
            P.drain("scalar")
            P.op("scalar", "activation", sb["t3"].ap(), sb["t1"].ap(),
                 AF.Sin, scale=(TWO_PI_N if is_sf else -TWO_PI_N),
                 bias=(sb["cols"].ap()[:, 30:31] if is_sf else
                       sb["cols"].ap()[:, 29:30]),
                 inc=[("sC", 1)])
            P.wait("vector", "sC", P.cnt["sC"])
            if is_sf:
                P.op("vector", "tensor_copy", sb["cols"].ap()[:, 8:9],
                     sb["pcol"].ap())
                for sft in (64.0, 32.0, 16.0, 8.0, 4.0, 2.0):
                    P.drain("vector")
                    P.op("vector", "tensor_single_scalar",
                         sb["cols"].ap()[:, 10:11], sb["cols"].ap()[:, 8:9],
                         sft, ALU.is_ge)
                    P.drain("vector")
                    P.op("vector", "scalar_tensor_tensor",
                         sb["cols"].ap()[:, 8:9], sb["cols"].ap()[:, 10:11],
                         -sft, sb["cols"].ap()[:, 8:9], ALU.mult, ALU.add)
                P.drain("vector")
                P.op("vector", "tensor_scalar", sb["t3"].ap()[:, 0:1],
                     sb["cols"].ap()[:, 8:9], -2.0, 1.0, ALU.mult, ALU.add)
                P.drain("vector")
            P.wait_pool("vector", "dA")
            split_fp32_to_bf16(sb["t3"].ap(), sb["b16a"].ap(),
                               sb["b16b"].ap(), inc=[("vV", 1)])
            P.wait("scalar", "vV", P.cnt["vV"])
            nm = "sf" if is_sf else "cf"
            P.dma("dA", tfw[nm, "hi"].ap()[ec], sb["b16a"].ap())
            P.dma("dA", tfw[nm, "lo"].ap()[ec], sb["b16b"].ap())

    for is_si in (0, 1):
        for kc in range(NBH):
            if kc == 0:
                P.drain("vector")
                P.op("vector", "tensor_scalar", sb["xnf"].ap(), sb["nn"].ap(),
                     sb["pcol"].ap(), 0.0 if is_si else 256.0, ALU.mult,
                     ALU.add)
                fold_steps(sb["xnf"].ap(), sb["atile"].ap(),
                           (65536.0, 32768.0, 16384.0, 8192.0, 4096.0,
                            2048.0, 1024.0))
            else:
                P.drain("vector")
                P.op("vector", "tensor_tensor", sb["xnf"].ap(),
                     sb["xnf"].ap(), sb["xti0"].ap(), ALU.add)
                fold_steps(sb["xnf"].ap(), sb["atile"].ap(), (1024.0,))
            P.op("vector", "engine_nop", inc=[("vV", 1)])
            P.wait("scalar", "vV", P.cnt["vV"])
            P.drain("scalar")
            P.op("scalar", "activation", sb["atile"].ap(), sb["xnf"].ap(),
                 AF.Sin, scale=(TWO_PI_N if is_si else -TWO_PI_N),
                 bias=(sb["cols"].ap()[:, 30:31] if is_si else
                       sb["cols"].ap()[:, 29:30]),
                 inc=[("sC", 1)])
            P.wait("vector", "sC", P.cnt["sC"])
            if not is_si:
                P.op("vector", "memset", sb["wcol"].ap(), 2.0 / 1024.0)
                if kc == 0:
                    P.drain("vector")
                    P.op("vector", "memset", sb["wcol"].ap()[0:1, :],
                         1.0 / 1024.0)
                P.drain("vector")
                P.op("vector", "tensor_scalar_mul", sb["atile"].ap(),
                     sb["atile"].ap(), sb["wcol"].ap())
                P.drain("vector")
            else:
                P.op("vector", "tensor_single_scalar", sb["atile"].ap(),
                     sb["atile"].ap(), 2.0 / 1024.0, ALU.mult)
                P.drain("vector")
            P.wait_pool("vector", "dA")
            split_fp32_to_bf16(sb["atile"].ap(), sb["b16c"].ap(),
                               sb["b16d"].ap(), inc=[("vV", 1)])
            P.wait("scalar", "vV", P.cnt["vV"])
            nm = "si" if is_si else "ci"
            P.dma("dA", tin[nm, "hi"].ap()[kc], sb["b16c"].ap())
            P.dma("dA", tin[nm, "lo"].ap()[kc], sb["b16d"].ap())

    P.wait_pool("vector", "dA")
    P.op("vector", "tensor_scalar", sb["xnf"].ap()[0:1, :],
         sb["nn"].ap()[0:1, :], 512.0, 256.0, ALU.mult, ALU.add)
    fold_steps(sb["xnf"].ap()[0:1, :], sb["atile"].ap()[0:1, :],
               (262144.0, 131072.0, 65536.0, 32768.0, 16384.0, 8192.0,
                4096.0, 2048.0, 1024.0))
    P.op("vector", "engine_nop", inc=[("vV", 1)])
    P.wait("scalar", "vV", P.cnt["vV"])
    P.drain("scalar")
    P.op("scalar", "activation", sb["atile"].ap()[0:1, :],
         sb["xnf"].ap()[0:1, :], AF.Sin, scale=-TWO_PI_N,
         bias=sb["cols"].ap()[0:1, 29:30], inc=[("sC", 1)])
    P.wait("vector", "sC", P.cnt["sC"])
    P.op("vector", "tensor_single_scalar", sb["atile"].ap()[0:1, :],
         sb["atile"].ap()[0:1, :], 1.0 / 1024.0, ALU.mult)
    P.drain("vector")
    split_fp32_to_bf16(sb["atile"].ap()[0:1, :], sb["b16c"].ap()[0:1, :],
                       sb["b16d"].ap()[0:1, :], inc=[("vV", 1)])
    P.wait("scalar", "vV", P.cnt["vV"])
    P.dma("dA", tin["si", "hi"].ap()[0][0:1, :], sb["b16c"].ap()[0:1, :])
    P.dma("dA", tin["si", "lo"].ap()[0][0:1, :], sb["b16d"].ap()[0:1, :])

    toks_scr = [(f"dA{i}", P.cnt[f"dA{i}"]) for i in range(POOLS["dA"][1])
                if P.cnt[f"dA{i}"]]

    for t in toks_scr:
        P.wait_tok("sync", t)
    toks_tbl_sb = []
    for nm, half, dst in (("cf", "hi", "cfh"), ("cf", "lo", "cfl"),
                          ("sf", "hi", "sfh"), ("sf", "lo", "sfl")):
        toks_tbl_sb.append(P.dma(
            "dS", sb[dst].ap().rearrange("p (ec kb) -> p ec kb", ec=ND),
            tfw[nm, half].ap().rearrange("ec p kb -> p ec kb")))
    toks_prep = emit_fold_prep(toks_tbl_sb)

    emit_P1(0)

    slot_free = [0, 0]
    slot_loaded = [None, None]

    def load_qkv_slot(j, nm, bc):
        P.wait("sync", "tM", slot_free[j])
        t1_ = P.dma("dS", sb[f"wsl{j}h"].ap(), asp[nm, "hi"].ap()[bc])
        t2_ = P.dma("dS", sb[f"wsl{j}l"].ap(), asp[nm, "lo"].ap()[bc])
        slot_loaded[j] = (t1_, t2_)

    for tc in range(NTC):
        tt0 = tc * NTT
        if tc > 0:
            P.wait("sync", "tM", P.cnt["tM"])
            slot_free[0] = max(slot_free[0], P.cnt["tM"])
            slot_free[1] = max(slot_free[1], P.cnt["tM"])
            load_qkv_slot(0, "k", 0)
            load_qkv_slot(1, "k", 1)
            preloaded_k = True

        if tc == 0:
            for t in toks_prep:
                P.wait_tok("sync", t)
            preloaded_k = False

        def emit_P4_chunk(jb):
            P.wait("vector", "sC", P.cnt["sC"])
            if True:
                qr = sb["fq"].ap()[:, jb * TC:(jb + 1) * TC]
                qi = sb["fq"].ap()[:, (NBH + jb) * TC:(NBH + jb + 1) * TC]
                mr = sb["Mpl"].ap()[:, jb * TC:(jb + 1) * TC]
                mi = sb["Mpl"].ap()[:, (NBH + jb) * TC:(NBH + jb + 1) * TC]
                t1, t2 = sb["t1"].ap()[:, 0:TC], sb["t2"].ap()[:, 0:TC]
                t3, t4 = sb["t3"].ap()[:, 0:TC], sb["t4"].ap()[:, 0:TC]
                P.drain("vector")
                P.op("vector", "tensor_tensor", t1, mr, qr, ALU.mult)
                P.op("vector", "tensor_tensor", t2, mi, qi, ALU.mult)
                P.op("vector", "tensor_tensor", t3, mi, qr, ALU.mult)
                P.op("vector", "tensor_tensor", t4, mr, qi, ALU.mult)
                P.drain("vector")
                P.op("vector", "tensor_tensor", t1, t1, t2, ALU.add)
                P.op("vector", "tensor_tensor", t3, t3, t4, ALU.subtract)
                P.drain("vector")
                if jb == 0:
                    P.op("vector", "tensor_tensor", t1[0:1, :], mr[0:1, :],
                         qr[0:1, :], ALU.mult)
                    P.op("vector", "tensor_tensor", t3[0:1, :], mi[0:1, :],
                         qi[0:1, :], ALU.mult)
                    P.drain("vector")
                split_fp32_to_bf16(t1, sb["zh"].ap()[:, jb * TC:(jb + 1) * TC],
                                   sb["zl"].ap()[:, jb * TC:(jb + 1) * TC])
                P.drain("vector")
                split_fp32_to_bf16(
                    t3, sb["zh"].ap()[:, (NBH + jb) * TC:(NBH + jb + 1) * TC],
                    sb["zl"].ap()[:, (NBH + jb) * TC:(NBH + jb + 1) * TC],
                    inc=[("vV", 1)])

        def emit_P3():
            P.wait("vector", "sC", P.cnt["sC"])
            for jb in range(NBH):
                kr = sb["fk"].ap()[:, jb * TC:(jb + 1) * TC]
                ki = sb["fk"].ap()[:, (NBH + jb) * TC:(NBH + jb + 1) * TC]
                vr = sb["fv"].ap()[:, jb * TC:(jb + 1) * TC]
                vi = sb["fv"].ap()[:, (NBH + jb) * TC:(NBH + jb + 1) * TC]
                mr = sb["Mpl"].ap()[:, jb * TC:(jb + 1) * TC]
                mi = sb["Mpl"].ap()[:, (NBH + jb) * TC:(NBH + jb + 1) * TC]
                t1, t2 = sb["t1"].ap()[:, 0:TC], sb["t2"].ap()[:, 0:TC]
                t3, t4 = sb["t3"].ap()[:, 0:TC], sb["t4"].ap()[:, 0:TC]
                P.drain("vector")
                P.op("vector", "tensor_tensor", t1, kr, vr, ALU.mult)
                P.op("vector", "tensor_tensor", t2, ki, vi, ALU.mult)
                P.op("vector", "tensor_tensor", t3, kr, vi, ALU.mult)
                P.op("vector", "tensor_tensor", t4, ki, vr, ALU.mult)
                P.drain("vector")
                P.op("vector", "tensor_tensor", t1, t1, t2, ALU.subtract)
                P.op("vector", "tensor_tensor", t3, t3, t4, ALU.add)
                P.drain("vector")
                if jb == 0:
                    P.op("vector", "tensor_tensor", t1[0:1, :], kr[0:1, :],
                         vr[0:1, :], ALU.mult)
                    P.op("vector", "tensor_tensor", t3[0:1, :], ki[0:1, :],
                         vi[0:1, :], ALU.mult)
                    P.drain("vector")
                P.op("vector", "tensor_tensor_scan", mr,
                     sb["zeros"].ap()[:, 0:TC], t1,
                     sb["carry"].ap()[:, jb:jb + 1], ALU.mult, ALU.add)
                P.op("vector", "tensor_tensor_scan", mi,
                     sb["zeros"].ap()[:, 0:TC], t3,
                     sb["carry"].ap()[:, NBH + jb:NBH + jb + 1], ALU.mult,
                     ALU.add)
                P.drain("vector")
                P.op("vector", "tensor_copy", sb["carry"].ap()[:, jb:jb + 1],
                     mr[:, TC - 1:TC])
                P.op("vector", "tensor_copy",
                     sb["carry"].ap()[:, NBH + jb:NBH + jb + 1], mi[:, TC - 1:TC])
                P.drain("vector")


        for nm, fpl in (("k", "fk"), ("v", "fv"), ("q", "fq")):
            if nm == "q":
                emit_P3()
            for bc_i in range(2 * NBH):
                j = bc_i % 2
                if not (preloaded_k and nm == "k" and bc_i < 2):
                    load_qkv_slot(j, nm, bc_i)
                for t_ in slot_loaded[j]:
                    P.wait_tok("tensor", t_)
                bank = next_bank()
                pieces = []
                for dc in range(ND):
                    lh = sb[f"wsl{j}h"].ap()[:, dc * 128:(dc + 1) * 128]
                    ll = sb[f"wsl{j}l"].ap()[:, dc * 128:(dc + 1) * 128]
                    xh = sb["xnhT"].ap()[:, dc * TC:(dc + 1) * TC]
                    xl = sb["xnlT"].ap()[:, dc * TC:(dc + 1) * TC]
                    pieces += [(lh, xh), (lh, xl), (ll, xh)]
                tm = matmul_group(bank, pieces)
                slot_free[j] = tm
                P.wait("scalar", "tM", tm)
                P.drain("scalar")
                P.op("scalar", "copy",
                     sb[fpl].ap()[:, bc_i * TC:(bc_i + 1) * TC],
                     ps[bank].ap()[:, 0:TC], inc=[("sC", 1)])
                bank_evac_done(bank, "sC")
                if nm == "q" and bc_i >= NBH:
                    emit_P4_chunk(bc_i - NBH)

        vv_z = P.cnt["vV"]

        P.wait("sync", "vV", vv_z)
        P.wait("sync", "tM", P.cnt["tM"])
        itbl_toks = []
        for nm, half, dst in (("ci", "hi", "cih"), ("ci", "lo", "cil"),
                              ("si", "hi", "sih"), ("si", "lo", "sil")):
            itbl_toks.append(P.dma(
                "dS", sb[dst].ap().rearrange("p (kc n) -> p kc n", kc=NBH),
                tin[nm, half].ap().rearrange("kc p n -> p kc n")))
        for t in itbl_toks:
            P.wait_tok("tensor", t)
        P.wait("tensor", "vV", vv_z)
        spill_toks = []
        for tt in range(NTT):
            for nh in range(2):
                bank = next_bank()
                pieces = []
                for bc_i in range(2 * NBH):
                    is_im = bc_i >= NBH
                    jb = bc_i % NBH
                    zh = sb["zh"].ap()[:, bc_i * TC + tt * 128:
                                       bc_i * TC + (tt + 1) * 128]
                    zl = sb["zl"].ap()[:, bc_i * TC + tt * 128:
                                       bc_i * TC + (tt + 1) * 128]
                    th = sb["sih" if is_im else "cih"].ap()[
                        :, jb * D + nh * 512:jb * D + (nh + 1) * 512]
                    tl = sb["sil" if is_im else "cil"].ap()[
                        :, jb * D + nh * 512:jb * D + (nh + 1) * 512]
                    pieces += [(zh, th), (zh, tl), (zl, th)]
                tm = matmul_group(bank, pieces)
                P.wait("scalar", "tM", tm)
                P.wait_pool("scalar", "dA")
                P.drain("scalar")
                P.op("scalar", "copy",
                     sb["atile"].ap()[:, nh * 512:(nh + 1) * 512],
                     ps[bank].ap()[:, 0:512], inc=[("sC", 1)])
                bank_evac_done(bank, "sC")
            P.drain("scalar")
            spill_toks.append(P.dma("dA", attn_t[tt0 + tt], sb["atile"].ap()))
        tm_p5 = P.cnt["tM"]
        slot_free[1] = max(slot_free[1], tm_p5)

        pairs = ((sb["fsl1"].ap(), sb["fsl2"].ap()),
                 (sb["fsl0"].ap()[:, 0:D], sb["fsl0"].ap()[:, D:2 * D]))
        pair_free = [slot_free[0], slot_free[1]]
        for oc in range(NO):
            pj = oc % 2
            sl1, sl2 = pairs[pj]
            P.wait("gpsimd", "tM", pair_free[pj])
            dv1 = P.dma("dG", sl1, w1t.ap()[oc])
            dv2 = P.dma("dG", sl2, w2t.ap()[oc])
            P.wait_tok("tensor", dv1)
            P.wait_tok("tensor", dv2)
            res = []
            for wsl in (sl1, sl2):
                bank = next_bank()
                pieces = []
                for dc in range(ND):
                    wblk = wsl[:, dc * 128:(dc + 1) * 128]
                    xh = sb["xnhT"].ap()[:, dc * TC:(dc + 1) * TC]
                    xl = sb["xnlT"].ap()[:, dc * TC:(dc + 1) * TC]
                    pieces += [(wblk, xh), (wblk, xl)]
                res.append((bank, matmul_group(bank, pieces)))
            pair_free[pj] = res[-1][1]
            (b1, tm1), (b2, tm2) = res
            P.wait("scalar", "tM", tm1)
            P.wait("scalar", "vV", P.cnt["vV"])
            P.op("scalar", "activation", sb["t1"].ap()[:, 0:TC],
                 ps[b1].ap()[:, 0:TC], AF.Silu,
                 scale=sb["scl"].ap()[:, oc:oc + 1])
            P.drain("scalar")
            P.wait("scalar", "tM", tm2)
            P.op("scalar", "activation", sb["t2"].ap()[:, 0:TC],
                 ps[b2].ap()[:, 0:TC], AF.Identity,
                 scale=sb["scl"].ap()[:, NO + oc:NO + oc + 1],
                 inc=[("sC", 1)])
            P.wait("vector", "sC", P.cnt["sC"])
            P.drain("vector")
            P.op("vector", "tensor_tensor", sb["t3"].ap()[:, 0:TC],
                 sb["t1"].ap()[:, 0:TC], sb["t2"].ap()[:, 0:TC], ALU.mult)
            P.drain("vector")
            split_fp32_to_bf16(sb["t3"].ap()[:, 0:TC],
                               sb["arA_h"].ap()[:, oc * TC:(oc + 1) * TC],
                               sb["arA_l"].ap()[:, oc * TC:(oc + 1) * TC],
                               inc=[("vV", 1)])
            bank_evac_done(b1)
            bank_evac_done(b2)

        tm_after_p6 = P.cnt["tM"]

        half_free = [max(slot_free[1], pair_free[1]),
                     max(slot_free[1], pair_free[1])]
        for hb, src_h, src_l, dst_h, dst_l, scoff in (
                (h1t, "arA_h", "arA_l", "arB_h", "arB_l", 2 * NO),
                (h2t, "arB_h", "arB_l", "arA_h", "arA_l", 3 * NO)):
            vv_src = P.cnt["vV"]
            for oc in range(NO):
                dvh = []
                for h in range(2):
                    P.wait("gpsimd", "tM", half_free[h])
                    dvh.append(P.dma(
                        "dG", sb["fsl0"].ap()[:, h * 2048:(h + 1) * 2048],
                        hb.ap()[oc][:, h * 2048:(h + 1) * 2048]))
                bank = next_bank()
                P.wait("tensor", *ps_free[bank])
                P.wait("tensor", "vV", vv_src)
                for h in range(2):
                    P.wait_tok("tensor", dvh[h])
                    for kci in range(16):
                        kc = h * 16 + kci
                        wblk = sb["fsl0"].ap()[:, kc * 128:(kc + 1) * 128]
                        ah = sb[src_h].ap()[:, kc * TC:(kc + 1) * TC]
                        al = sb[src_l].ap()[:, kc * TC:(kc + 1) * TC]
                        P.op("tensor", "matmul", ps[bank].ap()[:, 0:TC],
                             wblk, ah, start=(kc == 0), stop=False)
                        lastall = kc == NO - 1
                        P.op("tensor", "matmul", ps[bank].ap()[:, 0:TC],
                             wblk, al, start=False, stop=lastall,
                             inc=[("tM", 1)] if kci == 15 else ())
                    half_free[h] = P.cnt["tM"]
                tm = P.cnt["tM"]
                P.wait("scalar", "tM", tm)
                P.wait("scalar", "vV", P.cnt["vV"])
                P.op("scalar", "activation", sb["t1"].ap()[:, 0:TC],
                     ps[bank].ap()[:, 0:TC], AF.Gelu,
                     scale=sb["scl"].ap()[:, scoff + oc:scoff + oc + 1],
                     inc=[("sC", 1)])
                P.wait("vector", "sC", P.cnt["sC"])
                split_fp32_to_bf16(sb["t1"].ap()[:, 0:TC],
                                   sb[dst_h].ap()[:, oc * TC:(oc + 1) * TC],
                                   sb[dst_l].ap()[:, oc * TC:(oc + 1) * TC],
                                   inc=[("vV", 1)])
                bank_evac_done(bank)

        if tc + 1 < NTC:
            emit_P1(tc + 1, overlap=True)

        slot_free[0] = pair_free[0]
        slot_free[1] = max(half_free)
        tm_h2_done = P.cnt["tM"]
        P.wait("gpsimd", "tM", tm_h2_done)
        w3_toks = []
        for nh in range(2):
            for q4 in range(4):
                off = nh * NO * 512 + q4 * NO * 128
                w3_toks.append(P.dma(
                    "dG", sb["w3pre"].ap()[:, off:off + NO * 128],
                    w3t.ap()[nh][:, q4 * NO * 128:(q4 + 1) * NO * 128]))
        vv_c = P.cnt["vV"]
        for t in w3_toks:
            P.wait_tok("tensor", t)

        pf_bufs = ((sb["xpf0"], sb["apf0"]), (sb["xti0"], sb["xti1"]))

        def pf_load(tt):
            xb, ab = pf_bufs[tt % 2]
            P.wait("sync", "vV", P.cnt["vV"])
            tx = P.dma("dS", xb.ap(), x_t[tt0 + tt])
            P.wait_tok("sync", spill_toks[tt])
            ta = P.dma("dS", ab.ap(), attn_t[tt0 + tt])
            return (tx, ta)

        pf_toks = [pf_load(0)]
        for tt in range(NTT):
            banks = [next_bank(), next_bank()]
            P.wait("tensor", "vV", vv_c)
            for nh in range(2):
                P.wait("tensor", *ps_free[banks[nh]])
            for kc in range(NO):
                ch = sb["arA_h"].ap()[:, kc * TC + tt * 128:
                                      kc * TC + (tt + 1) * 128]
                cl = sb["arA_l"].ap()[:, kc * TC + tt * 128:
                                      kc * TC + (tt + 1) * 128]
                for nh in range(2):
                    w = sb["w3pre"].ap()[:, (nh * NO + kc) * 512:
                                         (nh * NO + kc + 1) * 512]
                    last = kc == NO - 1
                    P.op("tensor", "matmul", ps[banks[nh]].ap()[:, 0:512],
                         ch, w, start=(kc == 0), stop=False)
                    P.op("tensor", "matmul", ps[banks[nh]].ap()[:, 0:512],
                         cl, w, start=False, stop=last,
                         inc=[("tM", 1)] if last and nh == 1 else ())
            tm_tt = P.cnt["tM"]
            if tt + 1 < NTT:
                pf_toks.append(pf_load(tt + 1))
            xb, ab = pf_bufs[tt % 2]
            tx, ta = pf_toks[tt]
            P.wait("vector", "tM", tm_tt)
            P.wait_tok("vector", tx)
            P.wait_tok("vector", ta)
            P.wait_pool("vector", "dA")
            for nh in range(2):
                seg = slice(nh * 512, (nh + 1) * 512)
                P.drain("vector")
                P.op("vector", "tensor_tensor", sb["t3"].ap()[:, 0:512],
                     ps[banks[nh]].ap()[:, 0:512], sb["s3bc"].ap()[:, seg],
                     ALU.mult)
                P.drain("vector")
                P.op("vector", "tensor_tensor", sb["t3"].ap()[:, 0:512],
                     sb["t3"].ap()[:, 0:512], ab.ap()[:, seg], ALU.add)
                P.drain("vector")
                P.op("vector", "tensor_tensor", sb["atile"].ap()[:, seg],
                     sb["t3"].ap()[:, 0:512], xb.ap()[:, seg],
                     ALU.add, inc=[("vV", 1)])
                bank_evac_done(banks[nh])
            P.wait("scalar", "vV", P.cnt["vV"])
            P.dma("dA", y_t[tt0 + tt], sb["atile"].ap())

    P.wait_pool("scalar", "dA")
    P.wait_pool("scalar", "dS")
    P.wait_pool("scalar", "dG")

    with nc.Block() as block:
        @block.tensor
        def _(e):
            for f in P.q["tensor"]:
                f(e)

        @block.vector
        def _(e):
            for f in P.q["vector"]:
                f(e)

        @block.scalar
        def _(e):
            for f in P.q["scalar"]:
                f(e)

        @block.sync
        def _(e):
            for f in P.q["sync"]:
                f(e)

        @block.gpsimd
        def _(e):
            for f in P.q["gpsimd"]:
                f(e)

    ctx.close()
    return nc


def host_layouts(inputs, T_TOT):
    f = np.float32

    def stat4(w):
        no_, ni = w.shape
        noc, nic = no_ // 128, ni // 128
        return np.ascontiguousarray(
            w.reshape(noc, 128, nic, 128).transpose(0, 3, 2, 1)).reshape(
            noc, 128, ni)

    com = {
        "wqT": np.ascontiguousarray(
            np.asarray(inputs["L0_attn_Wq"], f).reshape(ND, 128, D)),
        "wkT": np.ascontiguousarray(
            np.asarray(inputs["L0_attn_Wk"], f).reshape(ND, 128, D)),
        "wvT": np.ascontiguousarray(
            np.asarray(inputs["L0_attn_Wv"], f).reshape(ND, 128, D)),
        "w1t": stat4(np.asarray(inputs["L0_ffn_w1_w"], f)),
        "w2t": stat4(np.asarray(inputs["L0_ffn_w2_w"], f)),
        "h1t": stat4(np.asarray(inputs["L0_ffn_h1_w"], f)),
        "h2t": stat4(np.asarray(inputs["L0_ffn_h2_w"], f)),
        "w3t": np.ascontiguousarray(
            np.asarray(inputs["L0_ffn_w3_w"], f).T.reshape(NO, 128, 2, 512)
            .transpose(2, 1, 0, 3)).reshape(2, 128, NO * 512),
        "s1_in": np.ascontiguousarray(
            np.asarray(inputs["L0_ffn_w1_s"], f).reshape(NO, 128).T),
        "s2_in": np.ascontiguousarray(
            np.asarray(inputs["L0_ffn_w2_s"], f).reshape(NO, 128).T),
        "sh1_in": np.ascontiguousarray(
            np.asarray(inputs["L0_ffn_h1_s"], f).reshape(NO, 128).T),
        "sh2_in": np.ascontiguousarray(
            np.asarray(inputs["L0_ffn_h2_s"], f).reshape(NO, 128).T),
        "s3_in": np.ascontiguousarray(
            np.asarray(inputs["L0_ffn_w3_s"], f).reshape(1, D)),
    }
    x = np.asarray(inputs["L0_attn_x"], f)
    per_core = []
    for b in range(x.shape[0]):
        m = dict(com)
        m["x_in"] = np.ascontiguousarray(x[b, :T_TOT, :])
        per_core.append(m)
    return per_core


def kernel(**inputs):
    T_TOT, TC = 2048, 512
    nc = build_nc(T_TOT, TC)
    in_maps = host_layouts(inputs, T_TOT)
    res = run_bass_kernel_spmd(nc, in_maps, core_ids=list(range(8)))
    out = np.stack([np.asarray(res.results[i]["y"]) for i in range(8)], axis=0)
    return out.astype(np.float32)


# version 19
# speedup vs baseline: 1.1067x; 1.0259x over previous
import contextlib

import numpy as np

import concourse.bass as bass
import concourse.mybir as mybir
from concourse.bass_utils import run_bass_kernel_spmd

F32 = mybir.dt.float32
BF16 = mybir.dt.bfloat16
U8 = mybir.dt.uint8
AF = mybir.ActivationFunctionType
ALU = mybir.AluOpType
AX = mybir.AxisListType

D = 1024
HID = 4096
ND = D // 128
NO = HID // 128
NBH = 4
TWO_PI_N = float(2.0 * np.pi / 1024.0)
SIN_SCALE = np.float32(2.0 * np.pi / 1024.0)
while float(SIN_SCALE) * 512.0 > np.pi:
    SIN_SCALE = np.nextafter(SIN_SCALE, np.float32(0))
SIN_SCALE = float(SIN_SCALE)

ENGS = ("tensor", "vector", "scalar", "sync", "gpsimd")
POOLS = {"dS": ("sync", 6), "dA": ("scalar", 4), "dG": ("gpsimd", 4)}


class Plan:

    def __init__(self, sems):
        self.q = {e: [] for e in ENGS}
        self.sems = sems
        self.cnt = {s: 0 for s in sems}
        self.pool_idx = {p: 0 for p in POOLS}

    def op(self, eng, meth, *args, inc=(), **kw):
        inc = tuple(inc)

        def f(e, meth=meth, args=args, kw=kw, inc=inc):
            ins = getattr(e, meth)(*args, **kw)
            for s, d in inc:
                ins = ins.then_inc(self.sems[s], d)

        self.q[eng].append(f)
        for s, d in inc:
            self.cnt[s] += d

    def dma(self, pool, out_ap, in_ap, **kw):
        eng, n = POOLS[pool]
        s = f"{pool}{self.pool_idx[pool] % n}"
        self.pool_idx[pool] += 1
        self.wait(eng, s, self.cnt[s])
        self.op(eng, "dma_start", out_ap, in_ap, inc=[(s, 16)], **kw)
        return (s, self.cnt[s])

    def wait(self, eng, sem, val):
        def f(e, sem=sem, val=val):
            e.wait_ge(self.sems[sem], val)

        self.q[eng].append(f)

    def wait_tok(self, eng, tok):
        self.wait(eng, tok[0], tok[1])

    def wait_pool(self, eng, pool):
        _, n = POOLS[pool]
        for i in range(n):
            s = f"{pool}{i}"
            if self.cnt[s]:
                self.wait(eng, s, self.cnt[s])

    def drain(self, eng):
        self.q[eng].append(lambda e: e.drain())


def build_nc(T_TOT, TC):
    NTC = T_TOT // TC
    NTT = TC // 128

    nc = bass.Bass("TRN2", target_bir_lowering=False)

    x_in = nc.dram_tensor("x_in", [T_TOT, D], F32, kind="ExternalInput")
    wqT = nc.dram_tensor("wqT", [ND, 128, D], F32, kind="ExternalInput")
    wkT = nc.dram_tensor("wkT", [ND, 128, D], F32, kind="ExternalInput")
    wvT = nc.dram_tensor("wvT", [ND, 128, D], F32, kind="ExternalInput")
    w1t = nc.dram_tensor("w1t", [NO, 128, ND * 128], F32, kind="ExternalInput")
    w2t = nc.dram_tensor("w2t", [NO, 128, ND * 128], F32, kind="ExternalInput")
    h1t = nc.dram_tensor("h1t", [NO, 128, NO * 128], F32, kind="ExternalInput")
    h2t = nc.dram_tensor("h2t", [NO, 128, NO * 128], F32, kind="ExternalInput")
    w3t = nc.dram_tensor("w3t", [2, 128, NO * 512], F32,
                         kind="ExternalInput")
    s1_in = nc.dram_tensor("s1_in", [128, NO], F32,
                          kind="ExternalInput")
    s2_in = nc.dram_tensor("s2_in", [128, NO], F32,
                          kind="ExternalInput")
    sh1_in = nc.dram_tensor("sh1_in", [128, NO], F32,
                          kind="ExternalInput")
    sh2_in = nc.dram_tensor("sh2_in", [128, NO], F32,
                          kind="ExternalInput")
    s3_in = nc.dram_tensor("s3_in", [1, D], F32, kind="ExternalInput")

    y_out = nc.dram_tensor("y", [T_TOT, D], F32, kind="ExternalOutput")

    asp = {}
    for nm in ("q", "k", "v"):
        for half in ("hi", "lo"):
            asp[nm, half] = nc.dram_tensor(f"a{nm}_{half}",
                                           [2 * NBH, 128, D], BF16,
                                           kind="Internal")
    tfw = {}
    for nm in ("cf", "sf"):
        for half in ("hi", "lo"):
            tfw[nm, half] = nc.dram_tensor(f"{nm}_{half}", [ND, 128, 512],
                                           BF16, kind="Internal")
    tin = {}
    for nm in ("ci", "si"):
        for half in ("hi", "lo"):
            tin[nm, half] = nc.dram_tensor(f"{nm}_{half}", [NBH, 128, D],
                                           BF16, kind="Internal")
    attn_s = nc.dram_tensor("attn_s", [T_TOT, D], F32, kind="Internal")

    x_t = x_in.ap().rearrange("(tt p) d -> tt p d", p=128)
    y_t = y_out.ap().rearrange("(tt p) d -> tt p d", p=128)
    attn_t = attn_s.ap().rearrange("(tt p) d -> tt p d", p=128)

    ctx = contextlib.ExitStack()

    sb = {}

    def sb_alloc(name, shape, dtype):
        sb[name] = ctx.enter_context(nc.sbuf_tensor(name, shape, dtype))
        return sb[name]

    sb_alloc("idn", [128, 128], BF16)
    sb_alloc("xti0", [128, D], F32)
    sb_alloc("xti1", [128, D], F32)
    sb_alloc("xnf", [128, D], F32)
    sb_alloc("xh_tok", [128, D], BF16)
    sb_alloc("xl_tok", [128, D], BF16)
    sb_alloc("xnhT", [128, ND * TC], BF16)
    sb_alloc("xnlT", [128, ND * TC], BF16)
    sb_alloc("t1", [128, 512], F32)
    sb_alloc("t2", [128, 512], F32)
    sb_alloc("t3", [128, 512], F32)
    sb_alloc("t4", [128, 512], F32)
    sb_alloc("atile", [128, D], F32)
    sb_alloc("zeros", [128, 512], F32)
    sb_alloc("carry", [128, 8], F32)
    sb_alloc("cols", [128, 32], F32)
    sb_alloc("scl", [128, 4 * NO], F32)
    sb_alloc("s3bc", [128, D], F32)
    sb_alloc("kk", [128, 512], F32)
    sb_alloc("nn", [128, D], F32)
    sb_alloc("ncol", [128, 1], F32)
    sb_alloc("pcol", [128, 1], F32)
    sb_alloc("wcol", [128, 1], F32)
    sb_alloc("b16a", [128, 512], BF16)
    sb_alloc("b16b", [128, 512], BF16)
    sb_alloc("b16c", [128, D], BF16)
    sb_alloc("b16d", [128, D], BF16)

    POOLB = 140 * 1024
    pool_h = ctx.enter_context(nc.sbuf_tensor("pool", [128, POOLB], U8))
    pool_base = nc.lookup_mloc(pool_h).addr

    def pool_at(name, off_kib, shape, dtype):
        h = nc.virtual_sbuf_tensor(name, shape, dtype)
        nc._tensor(h.name, shape, dtype, type="SB")
        mloc = nc.lookup_mloc(h)
        mloc.allocated = True
        mloc.addr = pool_base + off_kib * 1024
        sb[name] = h
        return h

    pool_at("uh", 0, [128, ND * TC], BF16)
    pool_at("ul", 8, [128, ND * TC], BF16)
    pool_at("cfh", 16, [128, ND * 512], BF16)
    pool_at("cfl", 24, [128, ND * 512], BF16)
    pool_at("sfh", 32, [128, ND * 512], BF16)
    pool_at("sfl", 40, [128, ND * 512], BF16)
    pool_at("wsl0h", 48, [128, D], BF16)
    pool_at("wsl0l", 50, [128, D], BF16)
    pool_at("wsl1h", 52, [128, D], BF16)
    pool_at("wsl1l", 54, [128, D], BF16)
    pool_at("fq", 56, [128, 2 * NBH * TC], F32)
    pool_at("fk", 72, [128, 2 * NBH * TC], F32)
    pool_at("fv", 88, [128, 2 * NBH * TC], F32)
    pool_at("Mpl", 104, [128, 2 * NBH * TC], F32)
    pool_at("zh", 120, [128, 2 * NBH * TC], BF16)
    pool_at("zl", 128, [128, 2 * NBH * TC], BF16)
    pool_at("wfh", 56, [128, ND * D], BF16)
    pool_at("wfl", 72, [128, ND * D], BF16)
    pool_at("wgh", 88, [128, ND * D], BF16)
    pool_at("wgl", 104, [128, ND * D], BF16)
    pool_at("cih", 56, [128, NBH * D], BF16)
    pool_at("cil", 64, [128, NBH * D], BF16)
    pool_at("sih", 72, [128, NBH * D], BF16)
    pool_at("sil", 80, [128, NBH * D], BF16)
    pool_at("arA_h", 0, [128, NO * TC], BF16)
    pool_at("arA_l", 32, [128, NO * TC], BF16)
    pool_at("arB_h", 64, [128, NO * TC], BF16)
    pool_at("arB_l", 96, [128, NO * TC], BF16)
    pool_at("w3pre", 64, [128, 2 * NO * 512], BF16)
    pool_at("xpf0", 128, [128, D], F32)
    pool_at("apf0", 132, [128, D], F32)
    pool_at("fsl0", 128, [128, NO * 128], BF16)
    pool_at("fsl1", 136, [128, D], BF16)
    pool_at("fsl2", 138, [128, D], BF16)

    ps = [ctx.enter_context(nc.psum_tensor(f"ps{i}", [128, 512], F32))
          for i in range(6)]
    pstA = ctx.enter_context(nc.psum_tensor("pstA", [128, 1024], BF16))
    pstB = ctx.enter_context(nc.psum_tensor("pstB", [128, 1024], BF16))

    semnames = ["tM", "vV", "sC", "gP"]
    for p, (_, np_) in POOLS.items():
        semnames += [f"{p}{i}" for i in range(np_)]
    sems = {s: ctx.enter_context(nc.semaphore(s)) for s in semnames}
    P = Plan(sems)

    def split_fp32_to_bf16(src_ap, hi_ap, lo_ap, inc=()):
        P.op("vector", "tensor_copy", hi_ap, src_ap)
        P.drain("vector")
        P.op("vector", "tensor_tensor", lo_ap, src_ap, hi_ap, ALU.subtract,
             inc=inc)

    ps_free = [("vV", 0)] * 6

    def matmul_group(bank, pieces):
        P.wait("tensor", *ps_free[bank])
        width = pieces[0][1].shape[-1]
        n = len(pieces)
        for i, (l, r) in enumerate(pieces):
            P.op("tensor", "matmul", ps[bank].ap()[:, 0:width], l, r,
                 start=(i == 0), stop=(i == n - 1),
                 inc=[("tM", 1)] if i == n - 1 else ())
        return P.cnt["tM"]

    ps_rr = [0]

    def next_bank():
        i = ps_rr[0] % 6
        ps_rr[0] += 1
        return i

    def bank_evac_done(bank, sem="vV"):
        ps_free[bank] = (sem, P.cnt[sem])

    P.op("gpsimd", "iota", sb["t1"].ap()[:, 0:128], [[1, 128]], base=0,
         channel_multiplier=-1, allow_small_or_imprecise_dtypes=True)
    P.op("gpsimd", "iota", sb["kk"].ap(), [[1, 512]], base=0,
         channel_multiplier=0, allow_small_or_imprecise_dtypes=True)
    P.op("gpsimd", "iota", sb["nn"].ap(), [[1, D]], base=0,
         channel_multiplier=0, allow_small_or_imprecise_dtypes=True)
    P.op("gpsimd", "iota", sb["pcol"].ap(), [[0, 1]], base=0,
         channel_multiplier=1, allow_small_or_imprecise_dtypes=True,
         inc=[("gP", 1)])
    P.wait("vector", "gP", P.cnt["gP"])
    P.op("vector", "tensor_single_scalar", sb["idn"].ap(),
         sb["t1"].ap()[:, 0:128], 0.0, ALU.is_equal)
    P.op("vector", "memset", sb["carry"].ap(), 0.0)
    P.op("vector", "memset", sb["cols"].ap()[:, 28:29], 1e-5)
    P.op("vector", "memset", sb["cols"].ap()[:, 29:30], float(np.pi))
    P.op("vector", "memset", sb["cols"].ap()[:, 30:31], -float(np.pi))
    P.op("vector", "memset", sb["zeros"].ap(), 1.0, inc=[("vV", 1)])

    for i, src in enumerate((s1_in, s2_in, sh1_in, sh2_in)):
        P.dma("dS", sb["scl"].ap()[:, i * NO:(i + 1) * NO], src.ap())
    dv = P.dma("dS", sb["s3bc"].ap()[0:1, :], s3_in.ap())
    n = 1
    while n < 128:
        P.wait_tok("sync", dv)
        dv = P.dma("dS", sb["s3bc"].ap()[n:2 * n, :], sb["s3bc"].ap()[0:n, :])
        n *= 2

    def emit_P1_stats(tc, overlap=False):
        tt0 = tc * NTT
        if overlap:
            xb_tiles = tuple(sb["xti0"] if t % 2 == 0 else sb["xti1"]
                             for t in range(NTT))
        else:
            xb_tiles = (sb["xti0"], sb["xti1"], sb["xpf0"], sb["apf0"])
        dvs = [None] * NTT
        for tt in range(NTT):
            xbuf = xb_tiles[tt]
            if (not overlap) or tt < 2:
                P.wait("sync", "vV", P.cnt["vV"])
                P.wait("sync", "sC", P.cnt["sC"])
                dvs[tt] = P.dma("dS", xbuf.ap(), x_t[tt0 + tt])
        for tt in range(NTT):
            xbuf = xb_tiles[tt]
            if dvs[tt] is None:
                P.wait("sync", "vV", P.cnt["vV"])
                P.wait("sync", "sC", P.cnt["sC"])
                dvs[tt] = P.dma("dS", xbuf.ap(), x_t[tt0 + tt])
            P.wait_tok("vector", dvs[tt])
            P.drain("vector")
            P.op("vector", "reduce_sum", sb["cols"].ap()[:, tt:tt + 1],
                 xbuf.ap(), axis=AX.X, inc=[("vV", 1)])
            P.wait_tok("scalar", dvs[tt])
            if tt == 0:
                P.wait("scalar", "vV", P.cnt["vV"] - 1)
            P.drain("scalar")
            P.op("scalar", "activation", sb["xnf"].ap(), xbuf.ap(),
                 AF.Square, accum_out=sb["cols"].ap()[:, 4 + tt:5 + tt],
                 inc=[("sC", 1)])
        P.wait("vector", "sC", P.cnt["sC"])
        P.drain("vector")
        P.op("vector", "tensor_single_scalar", sb["cols"].ap()[:, 8:8 + NTT],
             sb["cols"].ap()[:, 0:NTT], 1.0 / D, ALU.mult)
        P.op("vector", "tensor_single_scalar", sb["cols"].ap()[:, 12:12 + NTT],
             sb["cols"].ap()[:, 4:4 + NTT], 1.0 / D, ALU.mult)
        P.drain("vector")
        P.op("vector", "tensor_tensor", sb["cols"].ap()[:, 16:16 + NTT],
             sb["cols"].ap()[:, 8:8 + NTT], sb["cols"].ap()[:, 8:8 + NTT],
             ALU.mult)
        P.drain("vector")
        P.op("vector", "tensor_tensor", sb["cols"].ap()[:, 12:12 + NTT],
             sb["cols"].ap()[:, 12:12 + NTT], sb["cols"].ap()[:, 16:16 + NTT],
             ALU.subtract, inc=[("vV", 1)])
        P.wait("scalar", "vV", P.cnt["vV"])
        P.drain("scalar")
        P.op("scalar", "activation", sb["cols"].ap()[:, 16:16 + NTT],
             sb["cols"].ap()[:, 12:12 + NTT], AF.Sqrt,
             bias=sb["cols"].ap()[:, 28:29], inc=[("sC", 1)])
        P.wait("vector", "sC", P.cnt["sC"])
        P.drain("vector")
        P.op("vector", "reciprocal", sb["cols"].ap()[:, 20:20 + NTT],
             sb["cols"].ap()[:, 16:16 + NTT])
        P.drain("vector")
        P.op("vector", "scalar_tensor_tensor", sb["cols"].ap()[:, 24:24 + NTT],
             sb["cols"].ap()[:, 8:8 + NTT], -1.0,
             sb["cols"].ap()[:, 20:20 + NTT], ALU.mult, ALU.mult,
             inc=[("vV", 1)])
        return xb_tiles

    def emit_P1_tile(tc, tt, xb_tiles, overlap=False):
        tt0 = tc * NTT
        xbuf = xb_tiles[tt]
        if overlap:
            P.wait("sync", "vV", P.cnt["vV"])
            P.wait("sync", "sC", P.cnt["sC"])
            rdv = P.dma("dS", xbuf.ap(), x_t[tt0 + tt])
            P.wait_tok("scalar", rdv)
            P.wait("scalar", "tM", tm_after_p6)
        P.wait("scalar", "vV", P.cnt["vV"])
        P.drain("scalar")
        P.op("scalar", "activation", sb["xnf"].ap(), xbuf.ap(),
             AF.Identity, bias=sb["cols"].ap()[:, 24 + tt:25 + tt],
             scale=sb["cols"].ap()[:, 20 + tt:21 + tt], inc=[("sC", 1)])
        P.wait("vector", "sC", P.cnt["sC"])
        P.drain("vector")
        split_fp32_to_bf16(sb["xnf"].ap(), sb["xh_tok"].ap(),
                           sb["xl_tok"].ap(), inc=[("vV", 1)])
        vv_split = P.cnt["vV"]
        P.wait("tensor", "vV", vv_split)
        sc_ev = P.cnt["sC"]
        i = 0
        for tok, dstT in ((sb["xh_tok"], sb["xnhT"]),
                          (sb["xl_tok"], sb["xnlT"])):
            for dc in range(ND):
                bankT = pstA if i % 2 == 0 else pstB
                if i >= 2:
                    P.wait("tensor", "sC", sc_ev + i - 1)
                P.op("tensor", "transpose", bankT.ap()[:, 0:128],
                     tok.ap()[:, dc * 128:(dc + 1) * 128],
                     sb["idn"].ap(), inc=[("tM", 1)])
                P.wait("scalar", "tM", P.cnt["tM"])
                P.op("scalar", "copy",
                     dstT.ap()[:, dc * TC + tt * 128:
                               dc * TC + (tt + 1) * 128],
                     bankT.ap()[:, 0:128], inc=[("sC", 1)])
                i += 1
        P.wait("tensor", "sC", P.cnt["sC"])

    def emit_P1(tc, overlap=False):
        xb = emit_P1_stats(tc, overlap)
        for tt in range(NTT):
            emit_P1_tile(tc, tt, xb, overlap)

    def emit_fold_prep(toks_tbl_sb):
        st_toks = []
        projs = (("k", wkT), ("v", wvT), ("q", wqT))
        pairs = ((sb["wfh"], sb["wfl"]), (sb["wgh"], sb["wgl"]))
        plane_free = [0, 0]
        vv_wd = {}

        def emit_wsplit(pi):
            nm, wt = projs[pi]
            ph, pl = pairs[pi % 2]
            P.wait("vector", "tM", plane_free[pi % 2])
            for ec in range(ND):
                xb = sb["xti0"] if ec % 2 == 0 else sb["xti1"]
                P.wait("sync", "vV", P.cnt["vV"])
                dv = P.dma("dS", xb.ap(), wt.ap()[ec])
                P.wait_tok("vector", dv)
                P.drain("vector")
                split_fp32_to_bf16(
                    xb.ap(), ph.ap()[:, ec * D:(ec + 1) * D],
                    pl.ap()[:, ec * D:(ec + 1) * D], inc=[("vV", 1)])
            vv_wd[pi] = P.cnt["vV"]

        emit_wsplit(0)
        for pi, (nm, wt) in enumerate(projs):
            if pi + 1 < len(projs):
                emit_wsplit(pi + 1)
            ph, pl = pairs[pi % 2]
            P.wait("tensor", "vV", vv_wd[pi])
            for t_ in toks_tbl_sb:
                P.wait_tok("tensor", t_)
            for dc in range(ND):
                for is_im in (0, 1):
                    th = sb["sfh" if is_im else "cfh"]
                    tl = sb["sfl" if is_im else "cfl"]
                    bank = next_bank()
                    pieces = []
                    for ec in range(ND):
                        wh = ph.ap()[:, ec * D + dc * 128:
                                     ec * D + (dc + 1) * 128]
                        wl = pl.ap()[:, ec * D + dc * 128:
                                     ec * D + (dc + 1) * 128]
                        tt_h = th.ap()[:, ec * 512:(ec + 1) * 512]
                        tt_l = tl.ap()[:, ec * 512:(ec + 1) * 512]
                        pieces += [(wh, tt_h), (wl, tt_h), (wh, tt_l)]
                    tm = matmul_group(bank, pieces)
                    P.wait("vector", "tM", tm)
                    for t_ in st_toks[-8:]:
                        P.wait_tok("vector", t_)
                    split_fp32_to_bf16(ps[bank].ap()[:, 0:512],
                                       sb["b16a"].ap(), sb["b16b"].ap(),
                                       inc=[("vV", 1)])
                    bank_evac_done(bank)
                    P.wait("scalar", "vV", P.cnt["vV"])
                    for bj in range(4):
                        bc = is_im * NBH + bj
                        st_toks.append(P.dma(
                            "dA",
                            asp[nm, "hi"].ap()[bc][:, dc * 128:
                                                   (dc + 1) * 128],
                            sb["b16a"].ap()[:, bj * 128:(bj + 1) * 128]))
                        st_toks.append(P.dma(
                            "dA",
                            asp[nm, "lo"].ap()[bc][:, dc * 128:
                                                   (dc + 1) * 128],
                            sb["b16b"].ap()[:, bj * 128:(bj + 1) * 128]))
            plane_free[pi % 2] = P.cnt["tM"]
        return [(f"dA{i}", P.cnt[f"dA{i}"]) for i in range(POOLS["dA"][1])
                if P.cnt[f"dA{i}"]]

    def fold_steps(tmp, mask, steps):
        for sft in steps:
            P.drain("vector")
            P.op("vector", "tensor_single_scalar", mask, tmp, sft, ALU.is_ge)
            P.drain("vector")
            P.op("vector", "scalar_tensor_tensor", tmp, mask, -sft, tmp,
                 ALU.mult, ALU.add)

    def fold_to_signed(tmp, mask):
        fold_steps(tmp, mask, (512.0,))
        P.drain("vector")
        P.op("vector", "scalar_tensor_tensor", tmp, mask, -512.0, tmp,
             ALU.mult, ALU.add)

    P.drain("vector")
    P.op("vector", "tensor_single_scalar", sb["t4"].ap(), sb["kk"].ap(),
         128.0, ALU.mult)
    fold_steps(sb["t4"].ap(), sb["t2"].ap(),
               (32768.0, 16384.0, 8192.0, 4096.0, 2048.0, 1024.0))
    P.drain("vector")
    P.op("vector", "tensor_single_scalar", sb["xti0"].ap(), sb["nn"].ap(),
         128.0, ALU.mult)
    fold_steps(sb["xti0"].ap(), sb["atile"].ap(),
               (65536.0, 32768.0, 16384.0, 8192.0, 4096.0, 2048.0, 1024.0))

    for is_sf in (0, 1):
        for ec in range(ND):
            if ec == 0:
                P.drain("vector")
                P.op("vector", "tensor_scalar", sb["t1"].ap(), sb["kk"].ap(),
                     sb["pcol"].ap(), 0.0 if is_sf else 256.0, ALU.mult,
                     ALU.add)
                fold_steps(sb["t1"].ap(), sb["t2"].ap(),
                           (65536.0, 32768.0, 16384.0, 8192.0, 4096.0,
                            2048.0, 1024.0))
            else:
                P.drain("vector")
                P.op("vector", "tensor_tensor", sb["t1"].ap(), sb["t1"].ap(),
                     sb["t4"].ap(), ALU.add)
                fold_steps(sb["t1"].ap(), sb["t2"].ap(), (1024.0,))
            P.drain("vector")
            P.op("vector", "tensor_scalar", sb["t3"].ap(), sb["t1"].ap(),
                 -1.0, 512.0, ALU.mult, ALU.add, inc=[("vV", 1)])
            P.wait("scalar", "vV", P.cnt["vV"])
            P.drain("scalar")
            P.op("scalar", "activation", sb["t3"].ap(), sb["t3"].ap(),
                 AF.Sin, scale=(-SIN_SCALE if is_sf else SIN_SCALE),
                 inc=[("sC", 1)])
            P.wait("vector", "sC", P.cnt["sC"])
            if is_sf:
                P.op("vector", "tensor_copy", sb["cols"].ap()[:, 8:9],
                     sb["pcol"].ap())
                for sft in (64.0, 32.0, 16.0, 8.0, 4.0, 2.0):
                    P.drain("vector")
                    P.op("vector", "tensor_single_scalar",
                         sb["cols"].ap()[:, 10:11], sb["cols"].ap()[:, 8:9],
                         sft, ALU.is_ge)
                    P.drain("vector")
                    P.op("vector", "scalar_tensor_tensor",
                         sb["cols"].ap()[:, 8:9], sb["cols"].ap()[:, 10:11],
                         -sft, sb["cols"].ap()[:, 8:9], ALU.mult, ALU.add)
                P.drain("vector")
                P.op("vector", "tensor_scalar", sb["t3"].ap()[:, 0:1],
                     sb["cols"].ap()[:, 8:9], -2.0, 1.0, ALU.mult, ALU.add)
                P.drain("vector")
            P.wait_pool("vector", "dA")
            split_fp32_to_bf16(sb["t3"].ap(), sb["b16a"].ap(),
                               sb["b16b"].ap(), inc=[("vV", 1)])
            P.wait("scalar", "vV", P.cnt["vV"])
            nm = "sf" if is_sf else "cf"
            P.dma("dA", tfw[nm, "hi"].ap()[ec], sb["b16a"].ap())
            P.dma("dA", tfw[nm, "lo"].ap()[ec], sb["b16b"].ap())

    for is_si in (0, 1):
        for kc in range(NBH):
            if kc == 0:
                P.drain("vector")
                P.op("vector", "tensor_scalar", sb["xnf"].ap(), sb["nn"].ap(),
                     sb["pcol"].ap(), 0.0 if is_si else 256.0, ALU.mult,
                     ALU.add)
                fold_steps(sb["xnf"].ap(), sb["atile"].ap(),
                           (65536.0, 32768.0, 16384.0, 8192.0, 4096.0,
                            2048.0, 1024.0))
            else:
                P.drain("vector")
                P.op("vector", "tensor_tensor", sb["xnf"].ap(),
                     sb["xnf"].ap(), sb["xti0"].ap(), ALU.add)
                fold_steps(sb["xnf"].ap(), sb["atile"].ap(), (1024.0,))
            P.drain("vector")
            P.op("vector", "tensor_scalar", sb["atile"].ap(), sb["xnf"].ap(),
                 -1.0, 512.0, ALU.mult, ALU.add, inc=[("vV", 1)])
            P.wait("scalar", "vV", P.cnt["vV"])
            P.drain("scalar")
            P.op("scalar", "activation", sb["atile"].ap(), sb["atile"].ap(),
                 AF.Sin, scale=(-SIN_SCALE if is_si else SIN_SCALE),
                 inc=[("sC", 1)])
            P.wait("vector", "sC", P.cnt["sC"])
            if not is_si:
                P.op("vector", "memset", sb["wcol"].ap(), 2.0 / 1024.0)
                if kc == 0:
                    P.drain("vector")
                    P.op("vector", "memset", sb["wcol"].ap()[0:1, :],
                         1.0 / 1024.0)
                P.drain("vector")
                P.op("vector", "tensor_scalar_mul", sb["atile"].ap(),
                     sb["atile"].ap(), sb["wcol"].ap())
                P.drain("vector")
            else:
                P.op("vector", "tensor_single_scalar", sb["atile"].ap(),
                     sb["atile"].ap(), 2.0 / 1024.0, ALU.mult)
                P.drain("vector")
            P.wait_pool("vector", "dA")
            split_fp32_to_bf16(sb["atile"].ap(), sb["b16c"].ap(),
                               sb["b16d"].ap(), inc=[("vV", 1)])
            P.wait("scalar", "vV", P.cnt["vV"])
            nm = "si" if is_si else "ci"
            P.dma("dA", tin[nm, "hi"].ap()[kc], sb["b16c"].ap())
            P.dma("dA", tin[nm, "lo"].ap()[kc], sb["b16d"].ap())

    P.wait_pool("vector", "dA")
    P.op("vector", "tensor_scalar", sb["xnf"].ap()[0:1, :],
         sb["nn"].ap()[0:1, :], 512.0, 256.0, ALU.mult, ALU.add)
    fold_steps(sb["xnf"].ap()[0:1, :], sb["atile"].ap()[0:1, :],
               (262144.0, 131072.0, 65536.0, 32768.0, 16384.0, 8192.0,
                4096.0, 2048.0, 1024.0))
    P.drain("vector")
    P.op("vector", "tensor_scalar", sb["atile"].ap()[0:1, :],
         sb["xnf"].ap()[0:1, :], -1.0, 512.0, ALU.mult, ALU.add,
         inc=[("vV", 1)])
    P.wait("scalar", "vV", P.cnt["vV"])
    P.drain("scalar")
    P.op("scalar", "activation", sb["atile"].ap()[0:1, :],
         sb["atile"].ap()[0:1, :], AF.Sin, scale=SIN_SCALE,
         inc=[("sC", 1)])
    P.wait("vector", "sC", P.cnt["sC"])
    P.op("vector", "tensor_single_scalar", sb["atile"].ap()[0:1, :],
         sb["atile"].ap()[0:1, :], 1.0 / 1024.0, ALU.mult)
    P.drain("vector")
    split_fp32_to_bf16(sb["atile"].ap()[0:1, :], sb["b16c"].ap()[0:1, :],
                       sb["b16d"].ap()[0:1, :], inc=[("vV", 1)])
    P.wait("scalar", "vV", P.cnt["vV"])
    P.dma("dA", tin["si", "hi"].ap()[0][0:1, :], sb["b16c"].ap()[0:1, :])
    P.dma("dA", tin["si", "lo"].ap()[0][0:1, :], sb["b16d"].ap()[0:1, :])

    toks_scr = [(f"dA{i}", P.cnt[f"dA{i}"]) for i in range(POOLS["dA"][1])
                if P.cnt[f"dA{i}"]]

    for t in toks_scr:
        P.wait_tok("sync", t)
    toks_tbl_sb = []
    for nm, half, dst in (("cf", "hi", "cfh"), ("cf", "lo", "cfl"),
                          ("sf", "hi", "sfh"), ("sf", "lo", "sfl")):
        toks_tbl_sb.append(P.dma(
            "dS", sb[dst].ap().rearrange("p (ec kb) -> p ec kb", ec=ND),
            tfw[nm, half].ap().rearrange("ec p kb -> p ec kb")))
    toks_prep = emit_fold_prep(toks_tbl_sb)

    emit_P1(0)

    slot_free = [0, 0]
    slot_loaded = [None, None]

    def load_qkv_slot(j, nm, bc):
        P.wait("sync", "tM", slot_free[j])
        t1_ = P.dma("dS", sb[f"wsl{j}h"].ap(), asp[nm, "hi"].ap()[bc])
        t2_ = P.dma("dS", sb[f"wsl{j}l"].ap(), asp[nm, "lo"].ap()[bc])
        slot_loaded[j] = (t1_, t2_)

    for tc in range(NTC):
        tt0 = tc * NTT
        if tc > 0:
            P.wait("sync", "tM", P.cnt["tM"])
            slot_free[0] = max(slot_free[0], P.cnt["tM"])
            slot_free[1] = max(slot_free[1], P.cnt["tM"])
            load_qkv_slot(0, "k", 0)
            load_qkv_slot(1, "k", 1)
            preloaded_k = True

        if tc == 0:
            for t in toks_prep:
                P.wait_tok("sync", t)
            preloaded_k = False

        def emit_P4_chunk(jb):
            P.wait("vector", "sC", P.cnt["sC"])
            if True:
                qr = sb["fq"].ap()[:, jb * TC:(jb + 1) * TC]
                qi = sb["fq"].ap()[:, (NBH + jb) * TC:(NBH + jb + 1) * TC]
                mr = sb["Mpl"].ap()[:, jb * TC:(jb + 1) * TC]
                mi = sb["Mpl"].ap()[:, (NBH + jb) * TC:(NBH + jb + 1) * TC]
                t1, t2 = sb["t1"].ap()[:, 0:TC], sb["t2"].ap()[:, 0:TC]
                t3, t4 = sb["t3"].ap()[:, 0:TC], sb["t4"].ap()[:, 0:TC]
                P.drain("vector")
                P.op("vector", "tensor_tensor", t1, mr, qr, ALU.mult)
                P.op("vector", "tensor_tensor", t2, mi, qi, ALU.mult)
                P.op("vector", "tensor_tensor", t3, mi, qr, ALU.mult)
                P.op("vector", "tensor_tensor", t4, mr, qi, ALU.mult)
                P.drain("vector")
                P.op("vector", "tensor_tensor", t1, t1, t2, ALU.add)
                P.op("vector", "tensor_tensor", t3, t3, t4, ALU.subtract)
                P.drain("vector")
                if jb == 0:
                    P.op("vector", "tensor_tensor", t1[0:1, :], mr[0:1, :],
                         qr[0:1, :], ALU.mult)
                    P.op("vector", "tensor_tensor", t3[0:1, :], mi[0:1, :],
                         qi[0:1, :], ALU.mult)
                    P.drain("vector")
                split_fp32_to_bf16(t1, sb["zh"].ap()[:, jb * TC:(jb + 1) * TC],
                                   sb["zl"].ap()[:, jb * TC:(jb + 1) * TC])
                P.drain("vector")
                split_fp32_to_bf16(
                    t3, sb["zh"].ap()[:, (NBH + jb) * TC:(NBH + jb + 1) * TC],
                    sb["zl"].ap()[:, (NBH + jb) * TC:(NBH + jb + 1) * TC],
                    inc=[("vV", 1)])

        def emit_P3():
            P.wait("vector", "sC", P.cnt["sC"])
            for jb in range(NBH):
                kr = sb["fk"].ap()[:, jb * TC:(jb + 1) * TC]
                ki = sb["fk"].ap()[:, (NBH + jb) * TC:(NBH + jb + 1) * TC]
                vr = sb["fv"].ap()[:, jb * TC:(jb + 1) * TC]
                vi = sb["fv"].ap()[:, (NBH + jb) * TC:(NBH + jb + 1) * TC]
                mr = sb["Mpl"].ap()[:, jb * TC:(jb + 1) * TC]
                mi = sb["Mpl"].ap()[:, (NBH + jb) * TC:(NBH + jb + 1) * TC]
                t1, t2 = sb["t1"].ap()[:, 0:TC], sb["t2"].ap()[:, 0:TC]
                t3, t4 = sb["t3"].ap()[:, 0:TC], sb["t4"].ap()[:, 0:TC]
                P.drain("vector")
                P.op("vector", "tensor_tensor", t1, kr, vr, ALU.mult)
                P.op("vector", "tensor_tensor", t2, ki, vi, ALU.mult)
                P.op("vector", "tensor_tensor", t3, kr, vi, ALU.mult)
                P.op("vector", "tensor_tensor", t4, ki, vr, ALU.mult)
                P.drain("vector")
                P.op("vector", "tensor_tensor", t1, t1, t2, ALU.subtract)
                P.op("vector", "tensor_tensor", t3, t3, t4, ALU.add)
                P.drain("vector")
                if jb == 0:
                    P.op("vector", "tensor_tensor", t1[0:1, :], kr[0:1, :],
                         vr[0:1, :], ALU.mult)
                    P.op("vector", "tensor_tensor", t3[0:1, :], ki[0:1, :],
                         vi[0:1, :], ALU.mult)
                    P.drain("vector")
                P.op("vector", "tensor_tensor_scan", mr,
                     sb["zeros"].ap()[:, 0:TC], t1,
                     sb["carry"].ap()[:, jb:jb + 1], ALU.mult, ALU.add)
                P.op("vector", "tensor_tensor_scan", mi,
                     sb["zeros"].ap()[:, 0:TC], t3,
                     sb["carry"].ap()[:, NBH + jb:NBH + jb + 1], ALU.mult,
                     ALU.add)
                P.drain("vector")
                P.op("vector", "tensor_copy", sb["carry"].ap()[:, jb:jb + 1],
                     mr[:, TC - 1:TC])
                P.op("vector", "tensor_copy",
                     sb["carry"].ap()[:, NBH + jb:NBH + jb + 1], mi[:, TC - 1:TC])
                P.drain("vector")


        for nm, fpl in (("k", "fk"), ("v", "fv"), ("q", "fq")):
            if nm == "q":
                emit_P3()
            for bc_i in range(2 * NBH):
                j = bc_i % 2
                if not (preloaded_k and nm == "k" and bc_i < 2):
                    load_qkv_slot(j, nm, bc_i)
                for t_ in slot_loaded[j]:
                    P.wait_tok("tensor", t_)
                bank = next_bank()
                pieces = []
                for dc in range(ND):
                    lh = sb[f"wsl{j}h"].ap()[:, dc * 128:(dc + 1) * 128]
                    ll = sb[f"wsl{j}l"].ap()[:, dc * 128:(dc + 1) * 128]
                    xh = sb["xnhT"].ap()[:, dc * TC:(dc + 1) * TC]
                    xl = sb["xnlT"].ap()[:, dc * TC:(dc + 1) * TC]
                    pieces += [(lh, xh), (lh, xl), (ll, xh)]
                tm = matmul_group(bank, pieces)
                slot_free[j] = tm
                P.wait("scalar", "tM", tm)
                P.drain("scalar")
                P.op("scalar", "copy",
                     sb[fpl].ap()[:, bc_i * TC:(bc_i + 1) * TC],
                     ps[bank].ap()[:, 0:TC], inc=[("sC", 1)])
                bank_evac_done(bank, "sC")
                if nm == "q" and bc_i >= NBH:
                    emit_P4_chunk(bc_i - NBH)

        vv_z = P.cnt["vV"]

        P.wait("sync", "vV", vv_z)
        P.wait("sync", "tM", P.cnt["tM"])
        itbl_toks = []
        for nm, half, dst in (("ci", "hi", "cih"), ("ci", "lo", "cil"),
                              ("si", "hi", "sih"), ("si", "lo", "sil")):
            itbl_toks.append(P.dma(
                "dS", sb[dst].ap().rearrange("p (kc n) -> p kc n", kc=NBH),
                tin[nm, half].ap().rearrange("kc p n -> p kc n")))
        for t in itbl_toks:
            P.wait_tok("tensor", t)
        P.wait("tensor", "vV", vv_z)
        spill_toks = []
        for tt in range(NTT):
            for nh in range(2):
                bank = next_bank()
                pieces = []
                for bc_i in range(2 * NBH):
                    is_im = bc_i >= NBH
                    jb = bc_i % NBH
                    zh = sb["zh"].ap()[:, bc_i * TC + tt * 128:
                                       bc_i * TC + (tt + 1) * 128]
                    zl = sb["zl"].ap()[:, bc_i * TC + tt * 128:
                                       bc_i * TC + (tt + 1) * 128]
                    th = sb["sih" if is_im else "cih"].ap()[
                        :, jb * D + nh * 512:jb * D + (nh + 1) * 512]
                    tl = sb["sil" if is_im else "cil"].ap()[
                        :, jb * D + nh * 512:jb * D + (nh + 1) * 512]
                    pieces += [(zh, th), (zh, tl), (zl, th)]
                tm = matmul_group(bank, pieces)
                P.wait("scalar", "tM", tm)
                P.wait_pool("scalar", "dA")
                P.drain("scalar")
                P.op("scalar", "copy",
                     sb["atile"].ap()[:, nh * 512:(nh + 1) * 512],
                     ps[bank].ap()[:, 0:512], inc=[("sC", 1)])
                bank_evac_done(bank, "sC")
            P.drain("scalar")
            spill_toks.append(P.dma("dA", attn_t[tt0 + tt], sb["atile"].ap()))
        tm_p5 = P.cnt["tM"]
        slot_free[1] = max(slot_free[1], tm_p5)

        pairs = ((sb["fsl1"].ap(), sb["fsl2"].ap()),
                 (sb["fsl0"].ap()[:, 0:D], sb["fsl0"].ap()[:, D:2 * D]))
        pair_free = [slot_free[0], slot_free[1]]
        for oc in range(NO):
            pj = oc % 2
            sl1, sl2 = pairs[pj]
            P.wait("gpsimd", "tM", pair_free[pj])
            dv1 = P.dma("dG", sl1, w1t.ap()[oc])
            dv2 = P.dma("dG", sl2, w2t.ap()[oc])
            P.wait_tok("tensor", dv1)
            P.wait_tok("tensor", dv2)
            res = []
            for wsl in (sl1, sl2):
                bank = next_bank()
                pieces = []
                for dc in range(ND):
                    wblk = wsl[:, dc * 128:(dc + 1) * 128]
                    xh = sb["xnhT"].ap()[:, dc * TC:(dc + 1) * TC]
                    xl = sb["xnlT"].ap()[:, dc * TC:(dc + 1) * TC]
                    pieces += [(wblk, xh), (wblk, xl)]
                res.append((bank, matmul_group(bank, pieces)))
            pair_free[pj] = res[-1][1]
            (b1, tm1), (b2, tm2) = res
            P.wait("scalar", "tM", tm1)
            P.wait("scalar", "vV", P.cnt["vV"])
            P.op("scalar", "activation", sb["t1"].ap()[:, 0:TC],
                 ps[b1].ap()[:, 0:TC], AF.Silu,
                 scale=sb["scl"].ap()[:, oc:oc + 1])
            P.drain("scalar")
            P.wait("scalar", "tM", tm2)
            P.op("scalar", "activation", sb["t2"].ap()[:, 0:TC],
                 ps[b2].ap()[:, 0:TC], AF.Identity,
                 scale=sb["scl"].ap()[:, NO + oc:NO + oc + 1],
                 inc=[("sC", 1)])
            P.wait("vector", "sC", P.cnt["sC"])
            P.drain("vector")
            P.op("vector", "tensor_tensor", sb["t3"].ap()[:, 0:TC],
                 sb["t1"].ap()[:, 0:TC], sb["t2"].ap()[:, 0:TC], ALU.mult)
            P.drain("vector")
            split_fp32_to_bf16(sb["t3"].ap()[:, 0:TC],
                               sb["arA_h"].ap()[:, oc * TC:(oc + 1) * TC],
                               sb["arA_l"].ap()[:, oc * TC:(oc + 1) * TC],
                               inc=[("vV", 1)])
            bank_evac_done(b1)
            bank_evac_done(b2)

        tm_after_p6 = P.cnt["tM"]

        half_free = [max(slot_free[1], pair_free[1]),
                     max(slot_free[1], pair_free[1])]
        for hb, src_h, src_l, dst_h, dst_l, scoff in (
                (h1t, "arA_h", "arA_l", "arB_h", "arB_l", 2 * NO),
                (h2t, "arB_h", "arB_l", "arA_h", "arA_l", 3 * NO)):
            vv_src = P.cnt["vV"]
            for oc in range(NO):
                dvh = []
                for h in range(2):
                    P.wait("gpsimd", "tM", half_free[h])
                    dvh.append(P.dma(
                        "dG", sb["fsl0"].ap()[:, h * 2048:(h + 1) * 2048],
                        hb.ap()[oc][:, h * 2048:(h + 1) * 2048]))
                bank = next_bank()
                P.wait("tensor", *ps_free[bank])
                P.wait("tensor", "vV", vv_src)
                for h in range(2):
                    P.wait_tok("tensor", dvh[h])
                    for kci in range(16):
                        kc = h * 16 + kci
                        wblk = sb["fsl0"].ap()[:, kc * 128:(kc + 1) * 128]
                        ah = sb[src_h].ap()[:, kc * TC:(kc + 1) * TC]
                        al = sb[src_l].ap()[:, kc * TC:(kc + 1) * TC]
                        P.op("tensor", "matmul", ps[bank].ap()[:, 0:TC],
                             wblk, ah, start=(kc == 0), stop=False)
                        lastall = kc == NO - 1
                        P.op("tensor", "matmul", ps[bank].ap()[:, 0:TC],
                             wblk, al, start=False, stop=lastall,
                             inc=[("tM", 1)] if kci == 15 else ())
                    half_free[h] = P.cnt["tM"]
                tm = P.cnt["tM"]
                P.wait("scalar", "tM", tm)
                P.wait("scalar", "vV", P.cnt["vV"])
                P.op("scalar", "activation", sb["t1"].ap()[:, 0:TC],
                     ps[bank].ap()[:, 0:TC], AF.Gelu,
                     scale=sb["scl"].ap()[:, scoff + oc:scoff + oc + 1],
                     inc=[("sC", 1)])
                P.wait("vector", "sC", P.cnt["sC"])
                split_fp32_to_bf16(sb["t1"].ap()[:, 0:TC],
                                   sb[dst_h].ap()[:, oc * TC:(oc + 1) * TC],
                                   sb[dst_l].ap()[:, oc * TC:(oc + 1) * TC],
                                   inc=[("vV", 1)])
                bank_evac_done(bank)

        if tc + 1 < NTC:
            emit_P1(tc + 1, overlap=True)

        slot_free[0] = pair_free[0]
        slot_free[1] = max(half_free)
        tm_h2_done = P.cnt["tM"]
        P.wait("gpsimd", "tM", tm_h2_done)
        w3_toks = []
        for nh in range(2):
            for q4 in range(4):
                off = nh * NO * 512 + q4 * NO * 128
                w3_toks.append(P.dma(
                    "dG", sb["w3pre"].ap()[:, off:off + NO * 128],
                    w3t.ap()[nh][:, q4 * NO * 128:(q4 + 1) * NO * 128]))
        vv_c = P.cnt["vV"]
        for t in w3_toks:
            P.wait_tok("tensor", t)

        pf_bufs = ((sb["xpf0"], sb["apf0"]), (sb["xti0"], sb["xti1"]))

        def pf_load(tt):
            xb, ab = pf_bufs[tt % 2]
            P.wait("sync", "vV", P.cnt["vV"])
            tx = P.dma("dS", xb.ap(), x_t[tt0 + tt])
            P.wait_tok("sync", spill_toks[tt])
            ta = P.dma("dS", ab.ap(), attn_t[tt0 + tt])
            return (tx, ta)

        pf_toks = [pf_load(0)]
        for tt in range(NTT):
            banks = [next_bank(), next_bank()]
            P.wait("tensor", "vV", vv_c)
            for nh in range(2):
                P.wait("tensor", *ps_free[banks[nh]])
            for kc in range(NO):
                ch = sb["arA_h"].ap()[:, kc * TC + tt * 128:
                                      kc * TC + (tt + 1) * 128]
                cl = sb["arA_l"].ap()[:, kc * TC + tt * 128:
                                      kc * TC + (tt + 1) * 128]
                for nh in range(2):
                    w = sb["w3pre"].ap()[:, (nh * NO + kc) * 512:
                                         (nh * NO + kc + 1) * 512]
                    last = kc == NO - 1
                    P.op("tensor", "matmul", ps[banks[nh]].ap()[:, 0:512],
                         ch, w, start=(kc == 0), stop=False)
                    P.op("tensor", "matmul", ps[banks[nh]].ap()[:, 0:512],
                         cl, w, start=False, stop=last,
                         inc=[("tM", 1)] if last and nh == 1 else ())
            tm_tt = P.cnt["tM"]
            if tt + 1 < NTT:
                pf_toks.append(pf_load(tt + 1))
            xb, ab = pf_bufs[tt % 2]
            tx, ta = pf_toks[tt]
            P.wait("vector", "tM", tm_tt)
            P.wait_tok("vector", tx)
            P.wait_tok("vector", ta)
            P.wait_pool("vector", "dA")
            for nh in range(2):
                seg = slice(nh * 512, (nh + 1) * 512)
                P.drain("vector")
                P.op("vector", "tensor_tensor", sb["t3"].ap()[:, 0:512],
                     ps[banks[nh]].ap()[:, 0:512], sb["s3bc"].ap()[:, seg],
                     ALU.mult)
                P.drain("vector")
                P.op("vector", "tensor_tensor", sb["t3"].ap()[:, 0:512],
                     sb["t3"].ap()[:, 0:512], ab.ap()[:, seg], ALU.add)
                P.drain("vector")
                P.op("vector", "tensor_tensor", sb["atile"].ap()[:, seg],
                     sb["t3"].ap()[:, 0:512], xb.ap()[:, seg],
                     ALU.add, inc=[("vV", 1)])
                bank_evac_done(banks[nh])
            P.wait("scalar", "vV", P.cnt["vV"])
            P.dma("dA", y_t[tt0 + tt], sb["atile"].ap())

    P.wait_pool("scalar", "dA")
    P.wait_pool("scalar", "dS")
    P.wait_pool("scalar", "dG")

    with nc.Block() as block:
        @block.tensor
        def _(e):
            for f in P.q["tensor"]:
                f(e)

        @block.vector
        def _(e):
            for f in P.q["vector"]:
                f(e)

        @block.scalar
        def _(e):
            for f in P.q["scalar"]:
                f(e)

        @block.sync
        def _(e):
            for f in P.q["sync"]:
                f(e)

        @block.gpsimd
        def _(e):
            for f in P.q["gpsimd"]:
                f(e)

    ctx.close()
    return nc


def host_layouts(inputs, T_TOT):
    f = np.float32

    def stat4(w):
        no_, ni = w.shape
        noc, nic = no_ // 128, ni // 128
        return np.ascontiguousarray(
            w.reshape(noc, 128, nic, 128).transpose(0, 3, 2, 1)).reshape(
            noc, 128, ni)

    com = {
        "wqT": np.ascontiguousarray(
            np.asarray(inputs["L0_attn_Wq"], f).reshape(ND, 128, D)),
        "wkT": np.ascontiguousarray(
            np.asarray(inputs["L0_attn_Wk"], f).reshape(ND, 128, D)),
        "wvT": np.ascontiguousarray(
            np.asarray(inputs["L0_attn_Wv"], f).reshape(ND, 128, D)),
        "w1t": stat4(np.asarray(inputs["L0_ffn_w1_w"], f)),
        "w2t": stat4(np.asarray(inputs["L0_ffn_w2_w"], f)),
        "h1t": stat4(np.asarray(inputs["L0_ffn_h1_w"], f)),
        "h2t": stat4(np.asarray(inputs["L0_ffn_h2_w"], f)),
        "w3t": np.ascontiguousarray(
            np.asarray(inputs["L0_ffn_w3_w"], f).T.reshape(NO, 128, 2, 512)
            .transpose(2, 1, 0, 3)).reshape(2, 128, NO * 512),
        "s1_in": np.ascontiguousarray(
            np.asarray(inputs["L0_ffn_w1_s"], f).reshape(NO, 128).T),
        "s2_in": np.ascontiguousarray(
            np.asarray(inputs["L0_ffn_w2_s"], f).reshape(NO, 128).T),
        "sh1_in": np.ascontiguousarray(
            np.asarray(inputs["L0_ffn_h1_s"], f).reshape(NO, 128).T),
        "sh2_in": np.ascontiguousarray(
            np.asarray(inputs["L0_ffn_h2_s"], f).reshape(NO, 128).T),
        "s3_in": np.ascontiguousarray(
            np.asarray(inputs["L0_ffn_w3_s"], f).reshape(1, D)),
    }
    x = np.asarray(inputs["L0_attn_x"], f)
    per_core = []
    for b in range(x.shape[0]):
        m = dict(com)
        m["x_in"] = np.ascontiguousarray(x[b, :T_TOT, :])
        per_core.append(m)
    return per_core


def kernel(**inputs):
    T_TOT, TC = 2048, 512
    nc = build_nc(T_TOT, TC)
    in_maps = host_layouts(inputs, T_TOT)
    res = run_bass_kernel_spmd(nc, in_maps, core_ids=list(range(8)))
    out = np.stack([np.asarray(res.results[i]["y"]) for i in range(8)], axis=0)
    return out.astype(np.float32)
